# version 07 (final)
# speedup vs baseline: 1.0935x; 1.0257x over previous
import numpy as np
import ml_dtypes
from contextlib import ExitStack

import concourse.bass as bass
import concourse.mybir as mybir
from concourse.bass_utils import run_bass_kernel_spmd

F32 = mybir.dt.float32
BF16 = mybir.dt.bfloat16
AF = mybir.ActivationFunctionType
ALU = mybir.AluOpType
AX = mybir.AxisListType

N = 4096
D = 1024
NH = 16
DH = 64
NCHUNK = 8
TT = 512
NTT = N // TT
BF = np.dtype(ml_dtypes.bfloat16)


class KB:
    def __init__(self, name="kb", fused=False):
        self.fused = fused
        self.nc = bass.Bass("TRN2", target_bir_lowering=False, detect_race_conditions=False)
        self._osb = self.nc.sbuf_tensor
        self._ops = self.nc.psum_tensor
        self._oas = self.nc.alloc_sbuf_tensor

        def _sb(nm, shape, dtype, **kw):
            self.uid += 1
            return self._osb(f"u{self.uid}_{nm}", shape, dtype, **kw)

        def _ps(nm, shape, dtype=F32, **kw):
            self.uid += 1
            return self._ops(f"u{self.uid}_{nm}", shape, dtype, **kw)

        def _as(nm, shape, dtype, **kw):
            self.uid += 1
            return self._oas(f"u{self.uid}_{nm}", shape, dtype, **kw)

        self.nc.sbuf_tensor = _sb
        self.nc.psum_tensor = _ps
        self.nc.alloc_sbuf_tensor = _as
        self.ctx = ExitStack()
        self._sems = {}
        self.cnt = {}
        self.dram = {}
        self.kinds = {}
        self.uid = 0
        self.nsec = 0
        self.gen = 0

    def sem(self, name):
        if name not in self._sems:
            self._sems[name] = self.ctx.enter_context(self.nc.semaphore(name))
            self.cnt[name] = 0
        return self._sems[name]

    def bump(self, name, n):
        self.sem(name)
        self.cnt[name] += n
        return self.cnt[name]

    def din(self, name, shape, dtype=F32):
        if name not in self.dram:
            self.dram[name] = self.nc.dram_tensor(name, list(shape), dtype,
                                                  kind="ExternalInput").ap()
            self.kinds[name] = "in"
        return self.dram[name]

    def dout(self, name, shape, dtype=F32):
        if name not in self.dram:
            self.dram[name] = self.nc.dram_tensor(name, list(shape), dtype,
                                                  kind="ExternalOutput").ap()
            self.kinds[name] = "out"
        return self.dram[name]

    def dtmp(self, name, shape, dtype=F32):
        if name not in self.dram:
            self.dram[name] = self.nc.dram_tensor(name, list(shape), dtype,
                                                  kind="Internal").ap()
        return self.dram[name]

    def barrier(self):
        self.nc.all_engine_barrier()

    def close(self):
        self.ctx.close()


class SemSet:

    ND, NE, NS = 20, 16, 12

    def __init__(self, kb, droles, eroles, sroles=()):
        self.kb = kb
        kb.nsec += 1
        off = kb.nsec
        g = kb.gen
        self.map = {}
        for i, k in enumerate(droles):
            self.map[k] = f"gd{g}_{(i + off) % self.ND}"
        for i, k in enumerate(eroles):
            self.map[k] = f"ge{g}_{(i + off) % self.NE}"
        for i, k in enumerate(sroles):
            self.map[k] = f"gs{g}_{(i + off) % self.NS}"
        roles = list(droles) + list(eroles) + list(sroles)
        self.sm = {k: kb.sem(self.map[k]) for k in roles}
        self.z = {k: kb.cnt[self.map[k]] for k in roles}

    def __getitem__(self, k):
        return self.sm[k]

    def set_total(self, k, val):
        self.kb.cnt[self.map[k]] = val


def make_consts():
    c = {}
    c["c_id32"] = np.eye(128, dtype=np.float32)
    c["c_id16"] = np.eye(128, dtype=np.float32)
    c["c_ones_r128"] = np.ones((1, 128), dtype=np.float32)
    c["c_ones_c128"] = np.ones((128, 1), dtype=np.float32)
    bd = np.zeros((128, 2), dtype=np.float32)
    bd[0:64, 0] = 1.0
    bd[64:128, 1] = 1.0
    c["c_bd2"] = bd
    c["c_arange8"] = np.broadcast_to(np.arange(8, dtype=np.float32), (128, 8)).copy()
    bd16 = np.zeros((1024, 16), dtype=np.float32)
    for h in range(16):
        bd16[h * 64:(h + 1) * 64, h] = 1.0
    c["c_bd16"] = bd16
    sgn = np.ones((128, 1), dtype=np.float32)
    sgn[0:32] = -1.0
    sgn[64:96] = -1.0
    c["c_sgn32"] = sgn
    bd2t = np.zeros((2, 128), dtype=np.float32)
    bd2t[0, 0:64] = 1.0
    bd2t[1, 64:128] = 1.0
    c["c_bd2T"] = bd2t
    bd33t = np.zeros((33, 128), dtype=np.float32)
    bd33t[0, 0:64] = 1.0
    bd33t[32, 64:128] = 1.0
    c["c_bd33T"] = bd33t
    c["c_bd2b"] = bd2t.astype(BF)
    c["c_bd33b"] = bd33t.astype(BF)
    c["c_narange8"] = np.broadcast_to(-np.arange(8, dtype=np.float32), (128, 8)).copy()

    i = np.arange(DH // 2, dtype=np.float32)
    theta = np.power(np.float32(10000.0), -2.0 * (i - 1.0) / DH).astype(np.float32)
    ang = np.arange(N, dtype=np.float32)[:, None] * theta[None, :]
    c["c_l1a_cos"] = np.ascontiguousarray(np.cos(ang).T)
    c["c_l1a_sin"] = np.ascontiguousarray(np.sin(ang).T)

    WINDOW = 256
    half = np.arange(0, DH, 2, dtype=np.float32)
    inv_freq = (1.0 / 10000.0 ** (half / DH)).astype(np.float32)
    t = np.arange(N, dtype=np.float32)
    freqs = t[:, None] * inv_freq[None, :]
    base_scale = ((half + 0.4 * DH) / (1.4 * DH)).astype(np.float32)
    power = (t - N // 2) / (WINDOW // 2)
    scale = base_scale[None, :] ** power[:, None]
    c["c_lf_cs"] = np.ascontiguousarray((np.cos(freqs) * scale).T)
    c["c_lf_ss"] = np.ascontiguousarray((np.sin(freqs) * scale).T)
    c["c_lf_ci"] = np.ascontiguousarray((np.cos(freqs) / scale).T)
    c["c_lf_si"] = np.ascontiguousarray((np.sin(freqs) / scale).T)

    p = np.arange(128)[:, None]
    f = np.arange(512)[None, :]
    m = np.zeros((4 * 128, 512), dtype=np.float32)
    for j in range(4):
        m[j * 128:(j + 1) * 128] = (f >= 128 * j + p)
    c["c_l1a_mask"] = m

    f2 = np.arange(512)[None, :]
    m2 = np.zeros((6 * 128, 512), dtype=np.float32)
    for jj, ktrel in enumerate(range(-2, 4)):
        d = f2 - p - 128 * ktrel
        m2[jj * 128:(jj + 1) * 128] = np.where((d >= 0) & (d <= 256), 0.0, -1e5)
    c["c_lf_mask"] = m2
    smax = scale.max(axis=1)
    sinvmax = (1.0 / scale).max(axis=1)
    sbf = np.zeros((1, 8), dtype=np.float32)
    for qt in range(8):
        tq = smax[qt * 512:(qt + 1) * 512].max()
        k0 = max(0, qt * 512 - 256)
        tk = sinvmax[k0:qt * 512 + 512].max()
        sbf[0, qt] = tq * tk
    c["c_lf_sbf"] = sbf
    return c


def make_host_arrays(inp):
    a = {}
    T = lambda x: np.ascontiguousarray(np.asarray(x).T)
    A = lambda x: np.ascontiguousarray(np.asarray(x))
    col = lambda x: np.asarray(x).reshape(-1, 1).copy()
    row = lambda x: np.asarray(x).reshape(1, -1).copy()

    a["w_p_WqT"] = T(inp["L0_attn_Wq"])
    a["w_p_WkT"] = T(inp["L0_attn_Wk"])
    a["w_p_WvT"] = T(inp["L0_attn_Wv"])
    a["w_p_WoT"] = T(inp["L0_attn_Wo"])
    a["w_p_Wq_nat"] = A(inp["L0_attn_Wq"])
    a["w_p_Wk_nat"] = A(inp["L0_attn_Wk"])
    a["w_p_bq_col"] = col(inp["L0_attn_bq"])
    a["w_p_bk_col"] = col(inp["L0_attn_bk"])
    a["w_p_bq_row"] = row(inp["L0_attn_bq"])
    a["w_p_bk_row"] = row(inp["L0_attn_bk"])
    a["w_p_bv_row"] = row(inp["L0_attn_bv"])
    a["w_p_bo_col"] = col(inp["L0_attn_bo"])
    proj = np.asarray(inp["L0_attn_proj"])
    P = np.zeros((D, D), dtype=np.float32)
    for h in range(NH):
        P[h * DH:(h + 1) * DH, h * DH:(h + 1) * DH] = proj[h].T
    a["w_p_P"] = P

    a["w_m_ln1w_col"] = col(inp["L0_ffn_ln1_w"])
    a["w_m_ln1b_col"] = col(inp["L0_ffn_ln1_b"])
    a["w_m_qkv"] = A(inp["L0_ffn_w_qkv"])
    a["w_m_proj"] = A(inp["L0_ffn_w_proj"])
    a["w_m_ln2w_col"] = col(inp["L0_ffn_ln2_w"])
    a["w_m_ln2b_col"] = col(inp["L0_ffn_ln2_b"])
    a["w_m_gate"] = A(inp["L0_ffn_gate_w"])
    a["w_m_gateb_row"] = row(inp["L0_ffn_gate_b"])
    a["w_m_w1"] = A(inp["L0_ffn_w1"])
    a["w_m_w2"] = A(inp["L0_ffn_w2"])

    Wqkv = np.asarray(inp["L1_attn_Wqkv"]).copy()
    perm = np.concatenate([np.arange(0, DH, 2), np.arange(1, DH, 2)])
    for h in range(NH):
        for s in range(2):
            base = s * D + h * DH
            Wqkv[:, base:base + DH] = Wqkv[:, base + perm]
    a["w_a_qkv"] = Wqkv
    a["w_a_out"] = A(inp["L1_attn_Wout"])
    a["w_a_bout_col"] = col(inp["L1_attn_bout"])

    a["w_f_qkv"] = A(inp["L1_ffn_Wqkv"])
    a["w_f_qs2_col"] = np.ascontiguousarray(
        np.tile(np.asarray(inp["L1_ffn_q_scale"]).reshape(6, DH, 1), (1, 2, 1)))
    a["w_f_ks2_col"] = np.ascontiguousarray(
        np.tile(np.asarray(inp["L1_ffn_k_scale"]).reshape(6, DH, 1), (1, 2, 1)))
    a["w_f_Wo"] = A(inp["L1_ffn_Wo"])
    a["w_f_lng_col"] = np.asarray(inp["L1_ffn_ln_g"]).reshape(6, D, 1).copy()
    a["w_f_lnb_col"] = np.asarray(inp["L1_ffn_ln_b"]).reshape(6, D, 1).copy()
    permW = []
    jo = 0
    while jo < 2730:
        w = min(128, 2730 - jo)
        permW.extend(range(jo, jo + w))
        permW.extend(range(2730 + jo, 2730 + jo + w))
        jo += w
    permW = np.asarray(permW)
    a["w_f_W1p"] = np.ascontiguousarray(np.asarray(inp["L1_ffn_W1"])[:, :, permW])
    a["w_f_b1p_col"] = np.ascontiguousarray(
        np.asarray(inp["L1_ffn_b1"])[:, permW].reshape(6, 5460, 1))
    a["w_f_W2"] = A(inp["L1_ffn_W2"])
    a["w_f_b2_col"] = np.asarray(inp["L1_ffn_b2"]).reshape(6, D, 1).copy()
    return a


def emit_ln(kb, *, src, out_f=None, out_b=None, gain_col=None, bias_col=None,
            shift=False, eps=1e-5, name="ln"):
    nc = kb.nc
    sm = SemSet(kb, ["ld0", "ld1", "stf0", "stf1", "wl"],
                ["sq", "m1", "r1", "rs", "r2", "rp", "re", "av", "ag"],
                ["cl0", "cl1", "stb0", "stb1"])
    z = sm.z
    affine = gain_col is not None
    has_f = out_f is not None
    has_b = out_b is not None

    with ExitStack() as ctx:
        E = ctx.enter_context
        xf = E(nc.sbuf_tensor(name + "_xf", [128, 2 * NCHUNK * TT], F32))
        xb = E(nc.sbuf_tensor(name + "_xb", [128, 2 * NCHUNK * TT], BF16))
        sqb = E(nc.sbuf_tensor(name + "_sqb", [128, 2 * NCHUNK * TT], BF16))
        of = E(nc.sbuf_tensor(name + "_of", [128, 2 * NCHUNK * TT], F32))
        rows = E(nc.sbuf_tensor(name + "_rows", [1, 2 * 4 * TT], F32))
        arep = E(nc.sbuf_tensor(name + "_ar", [128, 2 * TT], F32))
        brep = E(nc.sbuf_tensor(name + "_br", [128, 2 * TT], F32))
        onesc = E(nc.sbuf_tensor(name + "_oc", [128, 1], BF16))
        onesr = E(nc.sbuf_tensor(name + "_or", [1, 128], F32))
        gcol = E(nc.sbuf_tensor(name + "_g", [128, NCHUNK], F32)) if affine else None
        bcol = E(nc.sbuf_tensor(name + "_bc", [128, NCHUNK], F32)) if affine else None
        ps_m = [E(nc.psum_tensor(name + f"_pm{p}", [1, TT], F32)) for p in range(2)]
        ps_v = [E(nc.psum_tensor(name + f"_pv{p}", [1, TT], F32)) for p in range(2)]
        ps_a = E(nc.psum_tensor(name + "_pa", [128, TT], F32))
        ps_b = E(nc.psum_tensor(name + "_pb", [128, TT], F32))

        XF = lambda p, c: xf[:, (p * NCHUNK + c) * TT:(p * NCHUNK + c) * TT + TT]
        XB = lambda p, c: xb[:, (p * NCHUNK + c) * TT:(p * NCHUNK + c) * TT + TT]
        SQ = lambda p, c: sqb[:, (p * NCHUNK + c) * TT:(p * NCHUNK + c) * TT + TT]
        OF = lambda p, c: of[:, (p * NCHUNK + c) * TT:(p * NCHUNK + c) * TT + TT]
        RW = lambda p, j: rows[0:1, (p * 4 + j) * TT:(p * 4 + j) * TT + TT]
        AR = lambda p: arep[:, p * TT:(p + 1) * TT]
        BR = lambda p: brep[:, p * TT:(p + 1) * TT]

        def src_ap(c, t):
            off = 1 if (shift and c >= NCHUNK // 2) else 0
            t0 = t * TT - off
            if t0 < 0:
                return src[c * 128:(c + 1) * 128, 0:TT - 1]
            return src[c * 128:(c + 1) * 128, t0:t0 + TT]

        def ldval(pfx, t):
            p = t % 2
            return z[pfx + str(p)] + 16 * NCHUNK * (t // 2 + 1)

        def emit_loads(eng, pfx, t, can_memset):
            p = t % 2
            dstf = XB if pfx == "cl" else XF
            s = sm[pfx + str(p)]
            for c in range(NCHUNK):
                dst = dstf(p, c)
                if shift and c >= NCHUNK // 2 and t == 0:
                    if can_memset:
                        eng.memset(dst[:, 0:1], 0.0)
                    eng.dma_start(out=dst[:, 1:TT], in_=src_ap(c, t)).then_inc(s, 16)
                else:
                    eng.dma_start(out=dst, in_=src_ap(c, t)).then_inc(s, 16)

        def stval(pfx, t):
            return z[pfx + str(t % 2)] + 16 * NCHUNK * (t // 2 + 1)

        with nc.Block() as blk:
            @blk.sync
            def _(sy):
                for t in range(NTT):
                    if t >= 2:
                        sy.wait_ge(sm["av"], z["av"] + t - 1)
                        sy.wait_ge(sm["ag"], z["ag"] + t - 1)
                    emit_loads(sy, "ld", t, False)
                    if has_f and t >= 1:
                        sy.wait_ge(sm["av"], z["av"] + t)
                        sy.wait_ge(sm["ag"], z["ag"] + t)
                        p = (t - 1) % 2
                        for c in range(NCHUNK):
                            sy.dma_start(out=out_f[c * 128:(c + 1) * 128, (t - 1) * TT:t * TT],
                                         in_=OF(p, c)).then_inc(sm["stf" + str(p)], 16)
                if has_f:
                    t = NTT - 1
                    sy.wait_ge(sm["av"], z["av"] + t + 1)
                    sy.wait_ge(sm["ag"], z["ag"] + t + 1)
                    p = t % 2
                    for c in range(NCHUNK):
                        sy.dma_start(out=out_f[c * 128:(c + 1) * 128, t * TT:(t + 1) * TT],
                                     in_=OF(p, c)).then_inc(sm["stf" + str(p)], 16)

            @blk.gpsimd
            def _(g):
                g.memset(onesc[:], 1.0)
                for t in range(NTT):
                    if t >= 2:
                        g.wait_ge(sm["m1"], z["m1"] + t - 1)
                    emit_loads(g, "cl", t, True)
                    for tv in ([t - 1] if t >= 1 else []) + ([NTT - 1] if t == NTT - 1 else []):
                        p = tv % 2
                        g.wait_ge(sm["re"], z["re"] + tv + 1)
                        g.wait_ge(sm["ld" + str(p)], ldval("ld", tv))
                        if tv >= 2:
                            if has_f:
                                g.wait_ge(sm["stf" + str(p)], stval("stf", tv - 2))
                            if has_b:
                                g.wait_ge(sm["stb" + str(p)], stval("stb", tv - 2))
                        for c in range(NCHUNK // 2, NCHUNK):
                            g.tensor_tensor(OF(p, c), XF(p, c), AR(p), ALU.mult)
                            ii = g.tensor_tensor(OF(p, c), OF(p, c), BR(p), ALU.add)
                            if affine:
                                ii = g.tensor_scalar(OF(p, c), OF(p, c), gcol[:, c:c + 1],
                                                     bcol[:, c:c + 1], ALU.mult, ALU.add)
                        ii.then_inc(sm["ag"], 1)
                        if has_b:
                            g.wait_ge(sm["av"], z["av"] + tv + 1)
                            for c in range(NCHUNK):
                                g.dma_start(out=out_b[c * 128:(c + 1) * 128, tv * TT:(tv + 1) * TT],
                                            in_=OF(p, c)).then_inc(sm["stb" + str(p)], 16)

            @blk.scalar
            def _(sc):
                if affine:
                    for c in range(NCHUNK):
                        sc.dma_start(out=gcol[:, c:c + 1],
                                     in_=gain_col[c * 128:(c + 1) * 128, 0:1]).then_inc(sm["wl"], 16)
                        sc.dma_start(out=bcol[:, c:c + 1],
                                     in_=bias_col[c * 128:(c + 1) * 128, 0:1]).then_inc(sm["wl"], 16)
                for t in range(NTT):
                    p = t % 2
                    sc.wait_ge(sm["cl" + str(p)], ldval("cl", t))
                    if t >= 2:
                        sc.wait_ge(sm["m1"], z["m1"] + t - 1)
                    for c in range(NCHUNK):
                        ii = sc.activation(SQ(p, c), XB(p, c), AF.Square, bias=0.0, scale=1.0)
                    ii.then_inc(sm["sq"], 1)
                    sc.wait_ge(sm["r1"], z["r1"] + t + 1)
                    sc.activation(RW(p, 2), RW(p, 1), AF.Ln, bias=0.0, scale=1.0)
                    sc.activation(RW(p, 2), RW(p, 2), AF.Exp, bias=0.0, scale=-0.5).then_inc(sm["rs"], 1)
                    sc.wait_ge(sm["rp"], z["rp"] + t + 1)
                    if t >= 2:
                        sc.wait_ge(sm["av"], z["av"] + t - 1)
                        sc.wait_ge(sm["ag"], z["ag"] + t - 1)
                    sc.copy(AR(p), ps_a[:])
                    sc.copy(BR(p), ps_b[:]).then_inc(sm["re"], 1)

            @blk.tensor
            def _(te):
                for t in range(NTT):
                    p = t % 2
                    te.wait_ge(sm["sq"], z["sq"] + t + 1)
                    if t >= 2:
                        te.wait_ge(sm["r1"], z["r1"] + t - 1)
                    for c in range(NCHUNK):
                        te.matmul(ps_m[p][:], onesc[:], XB(p, c), start=(c == 0),
                                  stop=(c == NCHUNK - 1))
                    for c in range(NCHUNK):
                        ii = te.matmul(ps_v[p][:], onesc[:], SQ(p, c), start=(c == 0),
                                       stop=(c == NCHUNK - 1))
                    ii.then_inc(sm["m1"], 1)
                    te.wait_ge(sm["r2"], z["r2"] + t + 1)
                    if t >= 1:
                        te.wait_ge(sm["re"], z["re"] + t)
                    te.matmul(ps_a[:], onesr[:], RW(p, 2), start=True, stop=True)
                    te.matmul(ps_b[:], onesr[:], RW(p, 3), start=True, stop=True).then_inc(sm["rp"], 1)

            @blk.vector
            def _(v):
                v.memset(onesr[:], 1.0)
                if shift:
                    for c in range(NCHUNK // 2, NCHUNK):
                        v.memset(XF(0, c)[:, 0:1], 0.0)
                for t in range(NTT):
                    p = t % 2
                    v.wait_ge(sm["m1"], z["m1"] + t + 1)
                    v.tensor_scalar_mul(RW(p, 0), ps_m[p][:], 1.0 / D)
                    v.tensor_tensor(RW(p, 3), RW(p, 0), RW(p, 0), ALU.mult)
                    v.tensor_scalar_mul(RW(p, 1), ps_v[p][:], 1.0 / D)
                    v.tensor_tensor(RW(p, 1), RW(p, 1), RW(p, 3), ALU.subtract)
                    v.tensor_scalar_add(RW(p, 1), RW(p, 1), float(eps)).then_inc(sm["r1"], 1)
                    v.wait_ge(sm["rs"], z["rs"] + t + 1)
                    v.tensor_tensor(RW(p, 3), RW(p, 0), RW(p, 2), ALU.mult)
                    v.tensor_scalar_mul(RW(p, 3), RW(p, 3), -1.0).then_inc(sm["r2"], 1)
                    v.wait_ge(sm["re"], z["re"] + t + 1)
                    v.wait_ge(sm["ld" + str(p)], ldval("ld", t))
                    if affine:
                        v.wait_ge(sm["wl"], z["wl"] + 16 * 2 * NCHUNK)
                    if t >= 2:
                        if has_f:
                            v.wait_ge(sm["stf" + str(p)], stval("stf", t - 2))
                        if has_b:
                            v.wait_ge(sm["stb" + str(p)], stval("stb", t - 2))
                    for c in range(NCHUNK // 2):
                        v.tensor_tensor(OF(p, c), XF(p, c), AR(p), ALU.mult)
                        ii = v.tensor_tensor(OF(p, c), OF(p, c), BR(p), ALU.add)
                        if affine:
                            ii = v.tensor_scalar(OF(p, c), OF(p, c), gcol[:, c:c + 1],
                                                 bcol[:, c:c + 1], ALU.mult, ALU.add)
                    ii.then_inc(sm["av"], 1)

        for p in range(2):
            nt_p = (NTT + 1 - p) // 2
            sm.set_total("ld" + str(p), z["ld" + str(p)] + 16 * NCHUNK * nt_p)
            sm.set_total("cl" + str(p), z["cl" + str(p)] + 16 * NCHUNK * nt_p)
            sm.set_total("stf" + str(p), z["stf" + str(p)] + (16 * NCHUNK * nt_p if has_f else 0))
            sm.set_total("stb" + str(p), z["stb" + str(p)] + (16 * NCHUNK * nt_p if has_b else 0))
        sm.set_total("wl", z["wl"] + (32 * NCHUNK if affine else 0))
        for k in ["sq", "m1", "r1", "rs", "r2", "rp", "re", "av", "ag"]:
            sm.set_total(k, z[k] + NTT)
        with nc.Block() as blk2:
            @blk2.sync
            def _(sy):
                for p in range(2):
                    sy.wait_ge(sm["stf" + str(p)], z["stf" + str(p)] + (16 * NCHUNK * ((NTT + 1 - p) // 2) if has_f else 0))
            @blk2.gpsimd
            def _(g):
                for p in range(2):
                    g.wait_ge(sm["stb" + str(p)], z["stb" + str(p)] + (16 * NCHUNK * ((NTT + 1 - p) // 2) if has_b else 0))
    kb.barrier()


def emit_proj_feat(kb, *, W, Dout, x, x_f32, out_b=None, out_f=None,
                   bias_col=None, resid=None, wscale_col=None, shift=False,
                   act=None, mask_mul=None, Din=D, NTOK=N, name="pf"):
    nc = kb.nc
    assert not (act is not None and bias_col is not None)
    NCH = (Din + 127) // 128
    wof = lambda i: min(128, Din - i * 128)
    n_o = Dout // 128
    ntt = NTOK // TT
    NPS = 3 if mask_mul is not None else 4
    NOB = 4
    has_f = out_f is not None
    has_b = out_b is not None
    dr = ["bl", "ml"] + [f"rl{j}" for j in range(NOB)] + [f"sf{j}" for j in range(NOB)]
    sr = ["wl"] + [f"sb{j}" for j in range(NOB)]
    if x_f32:
        sr += ["xl0", "xl1"]
    else:
        dr += ["xl0", "xl1"]
    sm = SemSet(kb, dr, ["ws", "mm", "ev", "blm", "mq", "mr", "ae"], sr)
    z = sm.z

    with ExitStack() as ctx:
        E = ctx.enter_context
        wa = E(nc.sbuf_tensor(name + "_wa", [128, NCH * Dout], BF16))
        xa = E(nc.sbuf_tensor(name + "_xa", [128, 2 * NCH * TT], BF16))
        ra = E(nc.sbuf_tensor(name + "_ra", [128, NOB * TT], F32)) if resid is not None else None
        ob = E(nc.sbuf_tensor(name + "_ob", [128, NOB * TT], F32 if has_f else BF16))
        bcol = E(nc.sbuf_tensor(name + "_bc", [128, n_o], F32)) if bias_col is not None else None
        wcol = E(nc.sbuf_tensor(name + "_wc", [128, NCH], F32)) if wscale_col is not None else None
        ps = [E(nc.psum_tensor(name + f"_ps{j}", [128, TT], F32)) for j in range(NPS)]
        if mask_mul is not None:
            mrow = E(nc.sbuf_tensor(name + "_mrw", [1, 2 * TT], F32))
            mrep = E(nc.sbuf_tensor(name + "_mrp", [128, 2 * TT], F32))
            onesrm = E(nc.sbuf_tensor(name + "_orm", [1, 128], F32))
            psM = E(nc.psum_tensor(name + "_psM", [128, TT], F32))
        at = E(nc.sbuf_tensor(name + "_at", [128, NOB * TT], F32)) if (act is not None and mask_mul is not None) else None

        WT = lambda i, o: wa[:, i * Dout + o * 128: i * Dout + o * 128 + 128]
        XT = lambda p, c: xa[:, (p * NCH + c) * TT:(p * NCH + c) * TT + TT]
        RT = lambda g: ra[:, (g % NOB) * TT:((g % NOB) + 1) * TT]
        OT = lambda g: ob[:, (g % NOB) * TT:((g % NOB) + 1) * TT]

        def xsrc(c, t):
            off = 1 if (shift and c >= NCH // 2) else 0
            t0 = t * TT - off
            if t0 < 0:
                return x[c * 128:c * 128 + wof(c), 0:TT - 1]
            return x[c * 128:c * 128 + wof(c), t0:t0 + TT]

        xlval = lambda t: z["xl" + str(t % 2)] + 16 * NCH * (t // 2 + 1)
        rlval = lambda g: z["rl" + str(g % NOB)] + 16 * (g // NOB + 1)
        sfval = lambda g: z["sf" + str(g % NOB)] + 16 * (g // NOB + 1)
        sbval = lambda g: z["sb" + str(g % NOB)] + 16 * (g // NOB + 1)
        nbl = ((16 * n_o if bias_col is not None else 0)
               + (16 * NCH if wscale_col is not None else 0))

        def emit_xloads(eng, t, can_memset):
            p = t % 2
            if t >= 2:
                eng.wait_ge(sm["mm"], z["mm"] + (t - 1) * n_o)
            s = sm["xl" + str(p)]
            for c in range(NCH):
                dst = XT(p, c)[0:wof(c), :]
                if shift and c >= NCH // 2 and t == 0:
                    if can_memset:
                        eng.memset(dst[:, 0:1], 0.0)
                    eng.dma_start(out=dst[:, 1:TT], in_=xsrc(c, t)).then_inc(s, 16)
                else:
                    eng.dma_start(out=dst, in_=xsrc(c, t)).then_inc(s, 16)

        def emit_store_b(g, t):
            for o in range(n_o):
                g_ = t * n_o + o
                g.wait_ge(sm["ev"], z["ev"] + g_ + 1)
                g.dma_start(out=out_b[o * 128:(o + 1) * 128, t * TT:(t + 1) * TT],
                            in_=OT(g_)).then_inc(sm["sb" + str(g_ % NOB)], 16)

        def emit_store_f(sy, t):
            for o in range(n_o):
                g_ = t * n_o + o
                sy.wait_ge(sm["ev"], z["ev"] + g_ + 1)
                sy.dma_start(out=out_f[o * 128:(o + 1) * 128, t * TT:(t + 1) * TT],
                             in_=OT(g_)).then_inc(sm["sf" + str(g_ % NOB)], 16)

        with nc.Block() as blk:
            @blk.gpsimd
            def _(g):
                for i in range(NCH):
                    g.dma_start(out=wa[0:wof(i), i * Dout:(i + 1) * Dout],
                                in_=W[i * 128:i * 128 + wof(i), :]).then_inc(sm["wl"], 16)
                for t in range(ntt):
                    if x_f32:
                        emit_xloads(g, t, True)
                    if has_b and t >= 1:
                        emit_store_b(g, t - 1)
                if has_b:
                    emit_store_b(g, ntt - 1)

            @blk.sync
            def _(sy):
                for t in range(ntt):
                    if not x_f32:
                        emit_xloads(sy, t, False)
                    if has_f and t >= 1:
                        emit_store_f(sy, t - 1)
                if has_f:
                    emit_store_f(sy, ntt - 1)

            @blk.vector
            def _(v):
                if not x_f32 and shift:
                    for c in range(NCH // 2, NCH):
                        ii = v.memset(XT(0, c)[:, 0:1], 0.0)
                    ii.then_inc(sm["blm"], 1)

            @blk.scalar
            def _(sc):
                if act is not None:
                    for t in range(ntt):
                        if mask_mul is not None:
                            if t >= 2:
                                sc.wait_ge(sm["mq"], z["mq"] + t - 1)
                                sc.wait_ge(sm["ev"], z["ev"] + (t - 1) * n_o)
                            sc.dma_start(out=mrow[0:1, (t % 2) * TT:(t % 2) * TT + TT],
                                         in_=mask_mul[0:1, t * TT:(t + 1) * TT]).then_inc(sm["ml"], 16)
                            sc.wait_ge(sm["mq"], z["mq"] + t + 1)
                            sc.copy(mrep[:, (t % 2) * TT:(t % 2) * TT + TT], psM[:]).then_inc(sm["mr"], 1)
                        for o in range(n_o):
                            g_ = t * n_o + o
                            sc.wait_ge(sm["mm"], z["mm"] + g_ + 1)
                            dst = at[:, (g_ % NOB) * TT:((g_ % NOB) + 1) * TT] if mask_mul is not None else OT(g_)
                            if mask_mul is None and g_ >= NOB:
                                if has_f:
                                    sc.wait_ge(sm[f"sf{g_ % NOB}"], sfval(g_ - NOB))
                                if has_b:
                                    sc.wait_ge(sm[f"sb{g_ % NOB}"], sbval(g_ - NOB))
                            elif mask_mul is not None and g_ >= NOB:
                                sc.wait_ge(sm["ev"], z["ev"] + g_ - NOB + 1)
                            sc.activation(dst, ps[g_ % NPS][:], act,
                                          bias=bcol[:, o:o + 1] if bias_col is not None else 0.0,
                                          scale=1.0).then_inc(sm["ae"], 1)
                if bias_col is not None:
                    for o in range(n_o):
                        sc.dma_start(out=bcol[:, o:o + 1],
                                     in_=bias_col[o * 128:(o + 1) * 128, 0:1]).then_inc(sm["bl"], 16)
                if wscale_col is not None:
                    for c in range(NCH):
                        sc.dma_start(out=wcol[:, c:c + 1],
                                     in_=wscale_col[c * 128:(c + 1) * 128, 0:1]).then_inc(sm["bl"], 16)
                if resid is not None:
                    for t in range(ntt):
                        for o in range(n_o):
                            g_ = t * n_o + o
                            if g_ >= NOB:
                                sc.wait_ge(sm["ev"], z["ev"] + g_ - NOB + 1)
                            sc.dma_start(out=RT(g_),
                                         in_=resid[o * 128:(o + 1) * 128, t * TT:(t + 1) * TT]
                                         ).then_inc(sm["rl" + str(g_ % NOB)], 16)

            @blk.tensor
            def _(te):
                te.wait_ge(sm["wl"], z["wl"] + 16 * NCH)
                if wscale_col is not None:
                    te.wait_ge(sm["ws"], z["ws"] + 1)
                if not x_f32 and shift:
                    te.wait_ge(sm["blm"], z["blm"] + 1)
                for t in range(ntt):
                    p = t % 2
                    te.wait_ge(sm["xl" + str(p)], xlval(t))
                    if mask_mul is not None:
                        te.wait_ge(sm["ml"], z["ml"] + 16 * (t + 1))
                        if t >= 2:
                            te.wait_ge(sm["mr"], z["mr"] + t - 1)
                        te.matmul(psM[:], onesrm[:], mrow[0:1, (t % 2) * TT:(t % 2) * TT + TT],
                                  start=True, stop=True).then_inc(sm["mq"], 1)
                    for o in range(n_o):
                        g_ = t * n_o + o
                        if g_ >= NPS:
                            te.wait_ge(sm["ev"], z["ev"] + g_ - NPS + 1)
                        for i in range(NCH):
                            ii = te.matmul(ps[g_ % NPS][:], WT(i, o)[0:wof(i), :],
                                           XT(p, i)[0:wof(i), :],
                                           start=(i == 0), stop=(i == NCH - 1))
                        ii.then_inc(sm["mm"], 1)

            @blk.vector
            def _(v):
                if mask_mul is not None:
                    v.memset(onesrm[:], 1.0)
                if act is not None:
                    for t in range(ntt):
                        for o in range(n_o):
                            g_ = t * n_o + o
                            if mask_mul is not None:
                                v.wait_ge(sm["ae"], z["ae"] + g_ + 1)
                                v.wait_ge(sm["mr"], z["mr"] + t + 1)
                                if g_ >= NOB:
                                    if has_f:
                                        v.wait_ge(sm[f"sf{g_ % NOB}"], sfval(g_ - NOB))
                                    if has_b:
                                        v.wait_ge(sm[f"sb{g_ % NOB}"], sbval(g_ - NOB))
                                v.tensor_tensor(OT(g_), at[:, (g_ % NOB) * TT:((g_ % NOB) + 1) * TT],
                                                mrep[:, (t % 2) * TT:(t % 2) * TT + TT],
                                                ALU.mult).then_inc(sm["ev"], 1)
                            else:
                                v.wait_ge(sm["ae"], z["ae"] + g_ + 1)
                                v.nop().then_inc(sm["ev"], 1)
                    return
                if wscale_col is not None:
                    v.wait_ge(sm["wl"], z["wl"] + 16 * NCH)
                    v.wait_ge(sm["bl"], z["bl"] + nbl)
                    for i in range(NCH):
                        ii = v.tensor_scalar_mul(wa[:, i * Dout:(i + 1) * Dout],
                                                 wa[:, i * Dout:(i + 1) * Dout], wcol[:, i:i + 1])
                    ii.then_inc(sm["ws"], 1)
                if bias_col is not None:
                    v.wait_ge(sm["bl"], z["bl"] + nbl)
                for t in range(ntt):
                    for o in range(n_o):
                        g_ = t * n_o + o
                        v.wait_ge(sm["mm"], z["mm"] + g_ + 1)
                        if g_ >= NOB:
                            if has_f:
                                v.wait_ge(sm["sf" + str(g_ % NOB)], sfval(g_ - NOB))
                            if has_b:
                                v.wait_ge(sm["sb" + str(g_ % NOB)], sbval(g_ - NOB))
                        pt = ps[g_ % NPS][:]
                        if resid is not None:
                            v.wait_ge(sm["rl" + str(g_ % NOB)], rlval(g_))
                            ii = v.scalar_tensor_tensor(
                                OT(g_), pt,
                                bcol[:, o:o + 1] if bias_col is not None else 0.0,
                                RT(g_), ALU.add, ALU.add)
                        elif bias_col is not None:
                            ii = v.tensor_single_scalar(OT(g_), pt, bcol[:, o:o + 1], ALU.add)
                        else:
                            ii = v.tensor_copy(OT(g_), pt)
                        ii.then_inc(sm["ev"], 1)

        ntile = ntt * n_o
        for p in range(2):
            nt_p = (ntt + 1 - p) // 2
            sm.set_total("xl" + str(p), z["xl" + str(p)] + 16 * NCH * nt_p)
        njs = {}
        for j in range(NOB):
            njs[j] = (ntile - j + NOB - 1) // NOB
            sm.set_total(f"rl{j}", z[f"rl{j}"] + (16 * njs[j] if resid is not None else 0))
            sm.set_total(f"sf{j}", z[f"sf{j}"] + (16 * njs[j] if has_f else 0))
            sm.set_total(f"sb{j}", z[f"sb{j}"] + (16 * njs[j] if has_b else 0))
        sm.set_total("wl", z["wl"] + 16 * NCH)
        sm.set_total("ws", z["ws"] + (1 if wscale_col is not None else 0))
        sm.set_total("bl", z["bl"] + nbl)
        sm.set_total("blm", z["blm"] + (1 if (not x_f32 and shift) else 0))
        sm.set_total("ml", z["ml"] + (16 * ntt if mask_mul is not None else 0))
        sm.set_total("mq", z["mq"] + (ntt if mask_mul is not None else 0))
        sm.set_total("mr", z["mr"] + (ntt if mask_mul is not None else 0))
        sm.set_total("ae", z["ae"] + (ntile if act is not None else 0))
        sm.set_total("mm", z["mm"] + ntile)
        sm.set_total("ev", z["ev"] + ntile)
        with nc.Block() as blk3:
            @blk3.sync
            def _(sy):
                for j in range(NOB):
                    sy.wait_ge(sm[f"sf{j}"], z[f"sf{j}"] + (16 * njs[j] if has_f else 0))
            @blk3.gpsimd
            def _(g):
                for j in range(NOB):
                    g.wait_ge(sm[f"sb{j}"], z[f"sb{j}"] + (16 * njs[j] if has_b else 0))
    kb.barrier()


def emit_proj_tok(kb, *, W, Dout, x, x_f32, out, bias_row=None, shift=False, name="pt"):
    nc = kb.nc
    n_os = Dout // TT
    ntt = N // 128
    NPS = 4
    NOB = 4
    dr = [f"sb{j}" for j in range(NOB)]
    sr = ["wl", "bl"]
    if x_f32:
        sr += ["xl0", "xl1"]
    else:
        dr += ["xl0", "xl1"]
    sm = SemSet(kb, dr, ["mm", "ev"], sr)
    z = sm.z

    with ExitStack() as ctx:
        E = ctx.enter_context
        wa = E(nc.sbuf_tensor(name + "_wa", [128, NCHUNK * Dout], BF16))
        xa = E(nc.sbuf_tensor(name + "_xa", [128, 2 * NCHUNK * 128], BF16))
        ba = E(nc.sbuf_tensor(name + "_ba", [1, Dout], BF16)) if bias_row is not None else None
        onesr = E(nc.sbuf_tensor(name + "_or", [1, 128], BF16))
        ob = E(nc.sbuf_tensor(name + "_ob", [128, NOB * TT], BF16))
        ps = [E(nc.psum_tensor(name + f"_ps{j}", [128, TT], F32)) for j in range(NPS)]

        WS = lambda i, os_: wa[:, i * Dout + os_ * TT: i * Dout + (os_) * TT + TT]
        XT = lambda p, c: xa[:, (p * NCHUNK + c) * 128:(p * NCHUNK + c) * 128 + 128]
        OT = lambda g: ob[:, (g % NOB) * TT:((g % NOB) + 1) * TT]
        xlval = lambda t: z["xl" + str(t % 2)] + 16 * NCHUNK * (t // 2 + 1)
        sbval = lambda g: z["sb" + str(g % NOB)] + 16 * (g // NOB + 1)

        with nc.Block() as blk:
            @blk.gpsimd
            def _(g):
                for i in range(NCHUNK):
                    g.dma_start(out=wa[:, i * Dout:(i + 1) * Dout],
                                in_=W[i * 128:(i + 1) * 128, :]).then_inc(sm["wl"], 16)
                if bias_row is not None:
                    g.dma_start(out=ba[:], in_=bias_row).then_inc(sm["bl"], 16)
                if x_f32:
                    for t in range(ntt):
                        p = t % 2
                        if t >= 2:
                            g.wait_ge(sm["mm"], z["mm"] + (t - 1) * n_os)
                        for c in range(NCHUNK):
                            off = 1 if (shift and c >= NCHUNK // 2) else 0
                            t0 = t * 128 - off
                            if t0 < 0:
                                g.memset(XT(p, c)[:, 0:1], 0.0)
                                g.dma_start(out=XT(p, c)[:, 1:128],
                                            in_=x[c * 128:(c + 1) * 128, 0:127]
                                            ).then_inc(sm["xl" + str(p)], 16)
                            else:
                                g.dma_start(out=XT(p, c),
                                            in_=x[c * 128:(c + 1) * 128, t0:t0 + 128]
                                            ).then_inc(sm["xl" + str(p)], 16)

            def store_t(sy, tv):
                for os_ in range(n_os):
                    g_ = tv * n_os + os_
                    sy.wait_ge(sm["ev"], z["ev"] + g_ + 1)
                    sy.dma_start(out=out[tv * 128:(tv + 1) * 128, os_ * TT:(os_ + 1) * TT],
                                 in_=OT(g_)).then_inc(sm["sb" + str(g_ % NOB)], 16)

            @blk.sync
            def _(sy):
                for t in range(ntt):
                    p = t % 2
                    if not x_f32:
                        if t >= 2:
                            sy.wait_ge(sm["mm"], z["mm"] + (t - 1) * n_os)
                        for c in range(NCHUNK):
                            off = 1 if (shift and c >= NCHUNK // 2) else 0
                            t0 = t * 128 - off
                            if t0 < 0:
                                sy.dma_start(out=XT(p, c)[:, 1:128],
                                             in_=x[c * 128:(c + 1) * 128, 0:127]
                                             ).then_inc(sm["xl" + str(p)], 16)
                            else:
                                sy.dma_start(out=XT(p, c),
                                             in_=x[c * 128:(c + 1) * 128, t0:t0 + 128]
                                             ).then_inc(sm["xl" + str(p)], 16)
                    if t >= 1:
                        store_t(sy, t - 1)
                store_t(sy, ntt - 1)

            @blk.tensor
            def _(te):
                te.wait_ge(sm["wl"], z["wl"] + 16 * NCHUNK)
                if bias_row is not None:
                    te.wait_ge(sm["bl"], z["bl"] + 16)
                for t in range(ntt):
                    p = t % 2
                    te.wait_ge(sm["xl" + str(p)], xlval(t))
                    for os_ in range(n_os):
                        g_ = t * n_os + os_
                        if g_ >= NPS:
                            te.wait_ge(sm["ev"], z["ev"] + g_ - NPS + 1)
                        for i in range(NCHUNK):
                            last = (i == NCHUNK - 1) and bias_row is None
                            ii = te.matmul(ps[g_ % NPS][:], XT(p, i), WS(i, os_),
                                           start=(i == 0), stop=last)
                        if bias_row is not None:
                            ii = te.matmul(ps[g_ % NPS][:], onesr[:],
                                           ba[0:1, os_ * TT:(os_ + 1) * TT],
                                           start=False, stop=True)
                        ii.then_inc(sm["mm"], 1)

            @blk.vector
            def _(v):
                v.memset(onesr[:], 1.0)
                if shift:
                    for c in range(NCHUNK // 2, NCHUNK):
                        v.memset(xa[:, (0 * NCHUNK + c) * 128:(0 * NCHUNK + c) * 128 + 1], 0.0)
                for t in range(ntt):
                    for os_ in range(n_os):
                        g_ = t * n_os + os_
                        v.wait_ge(sm["mm"], z["mm"] + g_ + 1)
                        if g_ >= NOB:
                            v.wait_ge(sm["sb" + str(g_ % NOB)], sbval(g_ - NOB))
                        v.tensor_copy(OT(g_), ps[g_ % NPS][:]).then_inc(sm["ev"], 1)

        ntile = ntt * n_os
        for p in range(2):
            nt_p = (ntt + 1 - p) // 2
            sm.set_total("xl" + str(p), z["xl" + str(p)] + 16 * NCHUNK * nt_p)
        njs = {}
        for j in range(NOB):
            njs[j] = (ntile - j + NOB - 1) // NOB
            sm.set_total(f"sb{j}", z[f"sb{j}"] + 16 * njs[j])
        sm.set_total("wl", z["wl"] + 16 * NCHUNK)
        sm.set_total("bl", z["bl"] + (16 if bias_row is not None else 0))
        sm.set_total("mm", z["mm"] + ntile)
        sm.set_total("ev", z["ev"] + ntile)
        with nc.Block() as blk3:
            @blk3.sync
            def _(sy):
                for j in range(NOB):
                    sy.wait_ge(sm[f"sb{j}"], z[f"sb{j}"] + 16 * njs[j])
    kb.barrier()


def emit_add(kb, *, srcs, out_f=None, out_b=None, name="ad"):
    nc = kb.nc
    nsrc = len(srcs)
    sm = SemSet(kb, ["ld0", "ld1", "sf0", "sf1"], ["av", "ag"], ["sb0", "sb1"])
    z = sm.z
    has_f = out_f is not None
    has_b = out_b is not None

    with ExitStack() as ctx:
        E = ctx.enter_context
        xf = E(nc.sbuf_tensor(name + "_xf", [128, 2 * nsrc * NCHUNK * TT], F32))
        of = E(nc.sbuf_tensor(name + "_of", [128, 2 * NCHUNK * TT], F32))
        XT = lambda p, s, c: xf[:, ((p * nsrc + s) * NCHUNK + c) * TT:
                                ((p * nsrc + s) * NCHUNK + c) * TT + TT]
        OF = lambda p, c: of[:, (p * NCHUNK + c) * TT:(p * NCHUNK + c) * TT + TT]
        ldval = lambda t: z["ld" + str(t % 2)] + 16 * nsrc * NCHUNK * (t // 2 + 1)
        stv = lambda pfx, t: z[pfx + str(t % 2)] + 16 * NCHUNK * (t // 2 + 1)

        with nc.Block() as blk:
            @blk.sync
            def _(sy):
                for t in range(NTT):
                    p = t % 2
                    if t >= 2:
                        sy.wait_ge(sm["av"], z["av"] + t - 1)
                        sy.wait_ge(sm["ag"], z["ag"] + t - 1)
                    for s in range(nsrc):
                        for c in range(NCHUNK):
                            sy.dma_start(out=XT(p, s, c),
                                         in_=srcs[s][c * 128:(c + 1) * 128, t * TT:(t + 1) * TT]
                                         ).then_inc(sm["ld" + str(p)], 16)
                    if has_f and t >= 1:
                        sy.wait_ge(sm["av"], z["av"] + t)
                        sy.wait_ge(sm["ag"], z["ag"] + t)
                        pv = (t - 1) % 2
                        for c in range(NCHUNK):
                            sy.dma_start(out=out_f[c * 128:(c + 1) * 128, (t - 1) * TT:t * TT],
                                         in_=OF(pv, c)).then_inc(sm["sf" + str(pv)], 16)
                if has_f:
                    t = NTT - 1
                    sy.wait_ge(sm["av"], z["av"] + t + 1)
                    sy.wait_ge(sm["ag"], z["ag"] + t + 1)
                    for c in range(NCHUNK):
                        sy.dma_start(out=out_f[c * 128:(c + 1) * 128, t * TT:(t + 1) * TT],
                                     in_=OF(t % 2, c)).then_inc(sm["sf" + str(t % 2)], 16)

            def apply(eng, semk, crange, t):
                p = t % 2
                eng.wait_ge(sm["ld" + str(p)], ldval(t))
                if t >= 2:
                    if has_f:
                        eng.wait_ge(sm["sf" + str(p)], stv("sf", t - 2))
                    if has_b:
                        eng.wait_ge(sm["sb" + str(p)], stv("sb", t - 2))
                for c in crange:
                    ii = eng.tensor_tensor(OF(p, c), XT(p, 0, c), XT(p, 1, c), ALU.add)
                    for s in range(2, nsrc):
                        ii = eng.tensor_tensor(OF(p, c), OF(p, c), XT(p, s, c), ALU.add)
                ii.then_inc(sm[semk], 1)

            @blk.vector
            def _(v):
                for t in range(NTT):
                    apply(v, "av", range(NCHUNK // 2), t)

            @blk.gpsimd
            def _(g):
                for t in range(NTT):
                    apply(g, "ag", range(NCHUNK // 2, NCHUNK), t)
                    if has_b and t >= 1:
                        tv = t - 1
                        pv = tv % 2
                        g.wait_ge(sm["av"], z["av"] + tv + 1)
                        for c in range(NCHUNK):
                            g.dma_start(out=out_b[c * 128:(c + 1) * 128, tv * TT:(tv + 1) * TT],
                                        in_=OF(pv, c)).then_inc(sm["sb" + str(pv)], 16)
                if has_b:
                    t = NTT - 1
                    g.wait_ge(sm["av"], z["av"] + t + 1)
                    for c in range(NCHUNK):
                        g.dma_start(out=out_b[c * 128:(c + 1) * 128, t * TT:(t + 1) * TT],
                                    in_=OF(t % 2, c)).then_inc(sm["sb" + str(t % 2)], 16)

        for p in range(2):
            nt_p = (NTT + 1 - p) // 2
            sm.set_total("ld" + str(p), z["ld" + str(p)] + 16 * nsrc * NCHUNK * nt_p)
            sm.set_total("sf" + str(p), z["sf" + str(p)] + (16 * NCHUNK * nt_p if has_f else 0))
            sm.set_total("sb" + str(p), z["sb" + str(p)] + (16 * NCHUNK * nt_p if has_b else 0))
        sm.set_total("av", z["av"] + NTT)
        sm.set_total("ag", z["ag"] + NTT)
        with nc.Block() as blk2:
            @blk2.sync
            def _(sy):
                for p in range(2):
                    sy.wait_ge(sm["sf" + str(p)], z["sf" + str(p)] + (16 * NCHUNK * ((NTT + 1 - p) // 2) if has_f else 0))
            @blk2.gpsimd
            def _(g):
                for p in range(2):
                    g.wait_ge(sm["sb" + str(p)], z["sb" + str(p)] + (16 * NCHUNK * ((NTT + 1 - p) // 2) if has_b else 0))
    kb.barrier()


def emit_attn_norms(kb, *, qf, kf, negc_out, scale, bd16, ident, extra_margin=0.0,
                    name="an"):
    nc = kb.nc
    sm = SemSet(kb, ["ld0", "ld1"], ["sq", "mm", "rd", "fin", "tp"], ["cl", "st"])
    z = sm.z

    with ExitStack() as ctx:
        E = ctx.enter_context
        xa = E(nc.sbuf_tensor(name + "_xa", [128, 2 * NCHUNK * TT], BF16))
        sq = E(nc.sbuf_tensor(name + "_sq", [128, 2 * NCHUNK * TT], BF16))
        bd = E(nc.sbuf_tensor(name + "_bd", [128, NCHUNK * 16], BF16))
        idb = E(nc.sbuf_tensor(name + "_id", [128, 128], F32))
        onesr = E(nc.sbuf_tensor(name + "_or", [1, 128], F32))
        mx = E(nc.sbuf_tensor(name + "_mx", [16, 2 * NTT], F32))
        nrm = E(nc.sbuf_tensor(name + "_nr", [16, 6], F32))
        crow = E(nc.sbuf_tensor(name + "_cr", [1, 16], F32))
        negc = E(nc.sbuf_tensor(name + "_ng", [128, 16], F32))
        ps_s = [E(nc.psum_tensor(name + f"_pss{p}", [16, TT], F32)) for p in range(2)]
        ps_t = E(nc.psum_tensor(name + "_pst", [16, 16], F32))
        ps_r = E(nc.psum_tensor(name + "_psr", [128, 16], F32))

        XT = lambda p, c: xa[:, (p * NCHUNK + c) * TT:(p * NCHUNK + c) * TT + TT]
        SQT = lambda p, c: sq[:, (p * NCHUNK + c) * TT:(p * NCHUNK + c) * TT + TT]
        BD = lambda c: bd[:, c * 16:(c + 1) * 16]

        with nc.Block() as blk:
            @blk.gpsimd
            def _(g):
                for c in range(NCHUNK):
                    g.dma_start(out=BD(c), in_=bd16[c * 128:(c + 1) * 128, :]).then_inc(sm["cl"], 16)
                g.dma_start(out=idb[:], in_=ident).then_inc(sm["cl"], 16)

            @blk.sync
            def _(sy):
                for j in range(2 * NTT):
                    src = qf if j < NTT else kf
                    t = j % NTT
                    p = j % 2
                    if j >= 2:
                        sy.wait_ge(sm["mm"], z["mm"] + j - 1)
                    for c in range(NCHUNK):
                        sy.dma_start(out=XT(p, c),
                                     in_=src[c * 128:(c + 1) * 128, t * TT:(t + 1) * TT]
                                     ).then_inc(sm["ld" + str(p)], 16)

            @blk.scalar
            def _(sc):
                for j in range(2 * NTT):
                    p = j % 2
                    sc.wait_ge(sm["ld" + str(p)], z["ld" + str(p)] + 16 * NCHUNK * (j // 2 + 1))
                    if j >= 2:
                        sc.wait_ge(sm["mm"], z["mm"] + j - 1)
                    for c in range(NCHUNK):
                        ii = sc.activation(SQT(p, c), XT(p, c), AF.Square, bias=0.0, scale=1.0)
                    ii.then_inc(sm["sq"], 1)
                sc.wait_ge(sm["rd"], z["rd"] + 2 * NTT + 1)
                sc.activation(nrm[:, 2:4], nrm[:, 0:2], AF.Sqrt, bias=0.0, scale=1.0
                              ).then_inc(sm["fin"], 1)

            @blk.tensor
            def _(te):
                te.wait_ge(sm["cl"], z["cl"] + 16 * (NCHUNK + 1))
                for j in range(2 * NTT):
                    p = j % 2
                    te.wait_ge(sm["sq"], z["sq"] + j + 1)
                    if j >= 2:
                        te.wait_ge(sm["rd"], z["rd"] + j - 1)
                    for c in range(NCHUNK):
                        ii = te.matmul(ps_s[p][:], BD(c), SQT(p, c), start=(c == 0),
                                       stop=(c == NCHUNK - 1))
                    ii.then_inc(sm["mm"], 1)
                te.wait_ge(sm["fin"], z["fin"] + 2)
                te.transpose(ps_t[0:1, :], nrm[:, 5:6], idb[0:16, 0:16]).then_inc(sm["tp"], 1)
                te.wait_ge(sm["tp"], z["tp"] + 2)
                te.matmul(ps_r[:], onesr[:], crow[:], start=True, stop=True).then_inc(sm["tp"], 1)

            @blk.vector
            def _(v):
                v.memset(onesr[:], 1.0)
                for j in range(2 * NTT):
                    p = j % 2
                    v.wait_ge(sm["mm"], z["mm"] + j + 1)
                    v.reduce_max(mx[:, j:j + 1], ps_s[p][:], AX.X).then_inc(sm["rd"], 1)
                v.reduce_max(nrm[:, 0:1], mx[:, 0:NTT], AX.X)
                v.reduce_max(nrm[:, 1:2], mx[:, NTT:2 * NTT], AX.X).then_inc(sm["rd"], 1)
                v.wait_ge(sm["fin"], z["fin"] + 1)
                v.tensor_tensor(nrm[:, 4:5], nrm[:, 2:3], nrm[:, 3:4], ALU.mult)
                v.tensor_scalar(nrm[:, 5:6], nrm[:, 4:5], -float(scale), -float(extra_margin),
                                ALU.mult, ALU.add).then_inc(sm["fin"], 1)
                v.wait_ge(sm["tp"], z["tp"] + 1)
                v.tensor_copy(crow[:], ps_t[0:1, :]).then_inc(sm["tp"], 1)
                v.wait_ge(sm["tp"], z["tp"] + 3)
                v.tensor_copy(negc[:], ps_r[:]).then_inc(sm["fin"], 1)

            @blk.gpsimd
            def _(g):
                g.wait_ge(sm["fin"], z["fin"] + 3)
                g.dma_start(out=negc_out, in_=negc[:]).then_inc(sm["st"], 16)
                g.wait_ge(sm["st"], z["st"] + 16)

        for p in range(2):
            sm.set_total("ld" + str(p), z["ld" + str(p)] + 16 * NCHUNK * NTT)
        sm.set_total("cl", z["cl"] + 16 * (NCHUNK + 1))
        sm.set_total("sq", z["sq"] + 2 * NTT)
        sm.set_total("mm", z["mm"] + 2 * NTT)
        sm.set_total("rd", z["rd"] + 2 * NTT + 1)
        sm.set_total("fin", z["fin"] + 3)
        sm.set_total("tp", z["tp"] + 3)
        sm.set_total("st", z["st"] + 16)
    kb.barrier()


def emit_attn_dense(kb, *, kind, qf, kf, v_tok, negc, scale, bd2T, out_y=None,
                    masks=None, den_io=None, name="at"):
    nc = kb.nc
    NQI = 8
    aug = 1 if kind == "msa" else 0
    causal = kind == "l1num"
    sm = SemSet(kb, ["ql", "ncl", "dl"] + [f"so{j}" for j in range(4)],
                ["ms", "ex", "mka", "mkb", "mv", "fr", "rq", "rv", "fz", "clm"],
                ["cl"])
    z = sm.z

    def nkt(qi):
        return 4 * (qi + 1) if causal else 32

    cum_kt = {}
    cum_mk = {}
    tot_kt = 0
    tot_mk = 0
    for hp in range(8):
        for qi in range(NQI):
            cum_kt[(hp, qi)] = tot_kt
            cum_mk[(hp, qi)] = tot_mk
            tot_kt += nkt(qi)
            if causal:
                tot_mk += 4
    n_items = 2
    item_of = lambda hp, qi: (hp * NQI + qi) * n_items
    tot_items = 8 * NQI * n_items

    with ExitStack() as ctx:
        E = ctx.enter_context
        qa = E(nc.sbuf_tensor(name + "_qa", [128, N], BF16))
        ka = E(nc.sbuf_tensor(name + "_ka", [128, N], BF16))
        va = E(nc.sbuf_tensor(name + "_va", [128, 32 * 130], BF16)) if kind != "l1den" else None
        pb = E(nc.sbuf_tensor(name + "_pb", [128, 4 * TT], BF16))
        ngc = E(nc.sbuf_tensor(name + "_ng", [128, 16], F32))
        mka = E(nc.sbuf_tensor(name + "_mk", [128, 4 * TT], BF16)) if causal else None
        onesc = E(nc.sbuf_tensor(name + "_oc", [128, 1], BF16))
        rec = E(nc.sbuf_tensor(name + "_rc", [1, 8 * TT], F32)) if kind == "l1den" else None
        nr2 = 2 if kind == "l1num" else 33
        rh2 = 1 if kind == "l1num" else 32
        if kind != "l1den":
            rec2 = E(nc.sbuf_tensor(name + "_rc2", [nr2, 2 * TT], F32))
            reps = E(nc.sbuf_tensor(name + "_rp", [128, 2 * TT], F32))
            bd2 = E(nc.sbuf_tensor(name + "_b2", [nr2, 128], F32))
        ot = E(nc.sbuf_tensor(name + "_ot", [64, 4 * TT], BF16)) if kind != "l1den" else None
        psS = [[E(nc.psum_tensor(name + f"_psS{h}{j}", [128, TT], F32)) for j in range(2)]
               for h in range(2)]
        if kind == "l1den":
            psD = [E(nc.psum_tensor(name + f"_psD{h}", [1, TT], F32)) for h in range(2)]
            psY, psR = None, None
        else:
            psY = [E(nc.psum_tensor(name + f"_psY{h}", [64 + aug, TT], F32)) for h in range(2)]
            psR = E(nc.psum_tensor(name + "_psR", [128, TT], F32))

        n_pairs = tot_items // 2
        PB = lambda h, kt: pb[:, (h * 2 + kt % 2) * TT:(h * 2 + kt % 2) * TT + TT]
        VA = lambda h, kt: va[:, kt * 130 + h * 65: kt * 130 + h * 65 + 64 + aug]
        MK = lambda j: mka[:, j * TT:(j + 1) * TT]
        REC = lambda it: rec[0:1, (it % 8) * TT:(it % 8) * TT + TT]
        RC2 = lambda pr: rec2[0:nr2, (pr % 2) * TT:(pr % 2) * TT + TT]
        RCH = lambda h, pr: rec2[h * rh2:h * rh2 + 1, (pr % 2) * TT:(pr % 2) * TT + TT]
        REPH = lambda h, pr: reps[h * 64:(h + 1) * 64, (pr % 2) * TT:(pr % 2) * TT + TT]
        OTT = lambda it: ot[:, (it % 4) * TT:(it % 4) * TT + TT]
        nload_hp = 16 * (16 + (64 if kind != "l1den" else 0))

        def pv_wait(te_, hp, qi, kt, h):
            ck = cum_kt[(hp, qi)]
            if causal and kt >= 4 * qi:
                j = kt - 4 * qi
                key = "mka" if h == 0 else "mkb"
                te_.wait_ge(sm[key], z[key] + cum_mk[(hp, qi)] + j + 1)
            else:
                te_.wait_ge(sm["ex"], z["ex"] + 2 * (ck + kt) + 1 + h)

        with nc.Block() as blk:
            @blk.gpsimd
            def _(g):
                if causal:
                    for j in range(4):
                        g.dma_start(out=MK(j), in_=masks[j * 128:(j + 1) * 128, :]
                                    ).then_inc(sm["cl"], 16)
                ii = g.memset(onesc[:], 1.0)
                if aug:
                    for kt in range(32):
                        for h in range(2):
                            ii = g.memset(va[:, kt * 130 + h * 65 + 64: kt * 130 + h * 65 + 65], 1.0)
                ii.then_inc(sm["clm"], 1)
                if causal:
                    for hp in range(8):
                        for qi in range(NQI):
                            ck, cm = cum_kt[(hp, qi)], cum_mk[(hp, qi)]
                            for j in range(4):
                                kt = 4 * qi + j
                                g.wait_ge(sm["ex"], z["ex"] + 2 * (ck + kt) + 2)
                                g.tensor_tensor(PB(1, kt), PB(1, kt), MK(j), ALU.mult
                                                ).then_inc(sm["mkb"], 1)

            @blk.sync
            def _(sy):
                for hp in range(8):
                    if hp >= 1:
                        sy.wait_ge(sm["mv"], z["mv"] + cum_kt[(hp, 0)])
                        sy.wait_ge(sm["fz"], z["fz"] + item_of(hp, 0))
                    for c in range(NCHUNK):
                        sy.dma_start(out=qa[:, c * TT:(c + 1) * TT],
                                     in_=qf[hp * 128:(hp + 1) * 128, c * TT:(c + 1) * TT]
                                     ).then_inc(sm["ql"], 16)
                        sy.dma_start(out=ka[:, c * TT:(c + 1) * TT],
                                     in_=kf[hp * 128:(hp + 1) * 128, c * TT:(c + 1) * TT]
                                     ).then_inc(sm["ql"], 16)
                    if kind != "l1den":
                        for kt in range(32):
                            for h in range(2):
                                sy.dma_start(
                                    out=VA(h, kt)[:, 0:64],
                                    in_=v_tok[kt * 128:(kt + 1) * 128,
                                              (hp * 2 + h) * 64:(hp * 2 + h) * 64 + 64]
                                    ).then_inc(sm["ql"], 16)
                    for qi in range(NQI):
                        base = item_of(hp, qi)
                        for it in range(n_items):
                            h = it
                            gi = base + it
                            sy.wait_ge(sm["fz"], z["fz"] + gi + 1)
                            if kind == "l1den":
                                sy.dma_start(out=den_io[hp * 2 + h:hp * 2 + h + 1,
                                                        qi * TT:(qi + 1) * TT],
                                             in_=REC(gi)).then_inc(sm[f"so{gi % 4}"], 16)
                            else:
                                sy.dma_start(out=out_y[(hp * 2 + h) * 64:(hp * 2 + h) * 64 + 64,
                                                       qi * TT:(qi + 1) * TT],
                                             in_=OTT(gi)).then_inc(sm[f"so{gi % 4}"], 16)

            @blk.scalar
            def _(sc):
                sc.dma_start(out=ngc[:], in_=negc).then_inc(sm["ncl"], 16)
                if kind != "l1den":
                    sc.dma_start(out=bd2[:], in_=bd2T).then_inc(sm["ncl"], 16)
                ncl_tot = 16 if kind == "l1den" else 32
                sc.wait_ge(sm["cl"], z["cl"] + 16 * (4 if causal else 0))
                sc.wait_ge(sm["clm"], z["clm"] + 1)
                sc.wait_ge(sm["ncl"], z["ncl"] + ncl_tot)
                for hp in range(8):
                    for qi in range(NQI):
                        ck = cum_kt[(hp, qi)]
                        pr = hp * NQI + qi
                        if kind == "l1num":
                            if pr >= 2:
                                sc.wait_ge(sm["rq"], z["rq"] + pr - 1)
                            sc.dma_start(out=RC2(pr),
                                         in_=den_io[hp * 2:hp * 2 + 2,
                                                    qi * TT:(qi + 1) * TT]
                                         ).then_inc(sm["dl"], 16)
                        for kt in range(nkt(qi)):
                            for h in range(2):
                                sc.wait_ge(sm["ms"], z["ms"] + 2 * (ck + kt) + 1 + h)
                                if ck + kt >= 2:
                                    sc.wait_ge(sm["mv"], z["mv"] + ck + kt - 1)
                                sc.activation(PB(h, kt), psS[h][kt % 2][:], AF.Exp,
                                              bias=ngc[:, hp * 2 + h:hp * 2 + h + 1],
                                              scale=float(scale)).then_inc(sm["ex"], 1)
                        if kind != "l1den":
                            sc.wait_ge(sm["rq"], z["rq"] + pr + 1)
                            sc.copy(reps[:, (pr % 2) * TT:(pr % 2) * TT + TT],
                                    psR[:]).then_inc(sm["rv"], 1)

            @blk.tensor
            def _(te):
                te.wait_ge(sm["cl"], z["cl"] + 16 * (4 if causal else 0))
                te.wait_ge(sm["clm"], z["clm"] + 1)
                te.wait_ge(sm["ncl"], z["ncl"] + (16 if kind == "l1den" else 32))
                for hp in range(8):
                    te.wait_ge(sm["ql"], z["ql"] + nload_hp * (hp + 1))
                    for qi in range(NQI):
                        ck = cum_kt[(hp, qi)]
                        pr = hp * NQI + qi
                        for kt in range(nkt(qi)):
                            for h in range(2):
                                if ck + kt >= 2:
                                    te.wait_ge(sm["ex"], z["ex"] + 2 * (ck + kt - 2) + 1 + h)
                                te.matmul(psS[h][kt % 2][:],
                                          ka[h * 64:(h + 1) * 64, kt * 128:(kt + 1) * 128],
                                          qa[h * 64:(h + 1) * 64, qi * TT:(qi + 1) * TT],
                                          start=True, stop=True,
                                          tile_position=(h * 64, 0)).then_inc(sm["ms"], 1)
                            if item_of(hp, qi) >= n_items and kt == 0:
                                te.wait_ge(sm["fz"], z["fz"] + item_of(hp, qi))
                            first = kt == 0
                            last = kt == nkt(qi) - 1
                            for h in range(2):
                                pv_wait(te, hp, qi, kt, h)
                                if kind == "l1den":
                                    ii = te.matmul(psD[h][:], onesc[:], PB(h, kt),
                                                   start=first, stop=last)
                                else:
                                    ii = te.matmul(psY[h][:], VA(h, kt), PB(h, kt),
                                                   start=first, stop=last)
                            ii.then_inc(sm["mv"], 1)
                        if kind != "l1den":
                            if kind == "l1num":
                                te.wait_ge(sm["dl"], z["dl"] + 16 * (pr + 1))
                            else:
                                te.wait_ge(sm["fr"], z["fr"] + 2 * (pr + 1))
                            if pr >= 1:
                                te.wait_ge(sm["rv"], z["rv"] + pr)
                            te.matmul(psR[:], bd2[:], RC2(pr), start=True,
                                      stop=True).then_inc(sm["rq"], 1)

            @blk.vector
            def _(v):
                if kind == "msa":
                    v.memset(rec2[:], 0.0)
                for hp in range(8):
                    for qi in range(NQI):
                        ck, cm = cum_kt[(hp, qi)], cum_mk[(hp, qi)]
                        pr = hp * NQI + qi
                        if causal:
                            for j in range(4):
                                kt = 4 * qi + j
                                v.wait_ge(sm["ex"], z["ex"] + 2 * (ck + kt) + 1)
                                v.tensor_tensor(PB(0, kt), PB(0, kt), MK(j), ALU.mult
                                                ).then_inc(sm["mka"], 1)
                        for it in range(n_items):
                            h = it
                            gi = item_of(hp, qi) + it
                            if kind == "l1den":
                                v.wait_ge(sm["mv"], z["mv"] + ck + nkt(qi))
                                if gi >= 8:
                                    v.wait_ge(sm[f"so{gi % 4}"], z[f"so{gi % 4}"]
                                              + 16 * ((gi - 8) // 4 + 1))
                                v.reciprocal(REC(gi), psD[h][:]).then_inc(sm["fz"], 1)
                            elif kind == "msa":
                                v.wait_ge(sm["mv"], z["mv"] + ck + nkt(qi))
                                if pr >= 2 and h == 0:
                                    v.wait_ge(sm["rq"], z["rq"] + pr - 1)
                                v.reciprocal(RCH(h, pr), psY[h][64:65, :]).then_inc(sm["fr"], 1)
                        if kind != "l1den":
                            for it in range(n_items):
                                h = it
                                gi = item_of(hp, qi) + it
                                v.wait_ge(sm["mv"], z["mv"] + ck + nkt(qi))
                                v.wait_ge(sm["rv"], z["rv"] + pr + 1)
                                if gi >= 4:
                                    v.wait_ge(sm[f"so{gi % 4}"], z[f"so{gi % 4}"]
                                              + 16 * ((gi - 4) // 4 + 1))
                                v.tensor_tensor(OTT(gi), psY[h][0:64, :], REPH(h, pr),
                                                ALU.mult).then_inc(sm["fz"], 1)

        sm.set_total("ql", z["ql"] + nload_hp * 8)
        sm.set_total("ms", z["ms"] + 2 * tot_kt)
        sm.set_total("ex", z["ex"] + 2 * tot_kt)
        sm.set_total("mka", z["mka"] + tot_mk)
        sm.set_total("mkb", z["mkb"] + tot_mk)
        sm.set_total("mv", z["mv"] + tot_kt)
        sm.set_total("cl", z["cl"] + 16 * (4 if causal else 0))
        sm.set_total("clm", z["clm"] + 1)
        sm.set_total("ncl", z["ncl"] + (16 if kind == "l1den" else 32))
        if kind == "l1den":
            for k in ["fr", "rq", "rv", "dl"]:
                sm.set_total(k, z[k])
        else:
            sm.set_total("fr", z["fr"] + (tot_items if kind == "msa" else 0))
            sm.set_total("rq", z["rq"] + n_pairs)
            sm.set_total("rv", z["rv"] + n_pairs)
            sm.set_total("dl", z["dl"] + (16 * n_pairs if kind == "l1num" else 0))
        sm.set_total("fz", z["fz"] + tot_items)
        for j in range(4):
            sm.set_total(f"so{j}", z[f"so{j}"] + 16 * (tot_items // 4))
        with nc.Block() as blk2:
            @blk2.sync
            def _(sy):
                for j in range(4):
                    sy.wait_ge(sm[f"so{j}"], z[f"so{j}"] + 16 * (tot_items // 4))
    kb.barrier()
def emit_moe_gate(kb, *, gate_w, gate_b_row, x, ohT_out, ident, name="mg"):
    nc = kb.nc
    sm = SemSet(kb, ["xl0", "xl1"] + [f"so{j}" for j in range(2)],
                ["mm", "oh", "tp", "ev"], ["wl"])
    z = sm.z
    ntt = N // 128

    with ExitStack() as ctx:
        E = ctx.enter_context
        wa = E(nc.sbuf_tensor(name + "_wa", [128, NCHUNK * 8], F32))
        ba = E(nc.sbuf_tensor(name + "_ba", [1, 8], F32))
        onesr = E(nc.sbuf_tensor(name + "_or", [1, 128], F32))
        xa = E(nc.sbuf_tensor(name + "_xa", [128, 2 * NCHUNK * 128], F32))
        nm = E(nc.sbuf_tensor(name + "_nm", [128, 2], F32))
        sg = E(nc.sbuf_tensor(name + "_sg", [128, 2 * 8], F32))
        oh = E(nc.sbuf_tensor(name + "_oh", [128, 2 * 8], F32))
        idb = E(nc.sbuf_tensor(name + "_id", [128, 128], F32))
        ot = E(nc.sbuf_tensor(name + "_ot", [8, 2 * 128], F32))
        psL = [E(nc.psum_tensor(name + f"_psL{p}", [128, 8], F32)) for p in range(2)]
        psT = [E(nc.psum_tensor(name + f"_psT{p}", [8, 128], F32)) for p in range(2)]

        XT = lambda p, c: xa[:, (p * NCHUNK + c) * 128:(p * NCHUNK + c) * 128 + 128]
        SG = lambda p: sg[:, p * 8:(p + 1) * 8]
        OH = lambda p: oh[:, p * 8:(p + 1) * 8]
        OT_ = lambda p: ot[:, p * 128:(p + 1) * 128]
        xlval = lambda t: z["xl" + str(t % 2)] + 16 * NCHUNK * (t // 2 + 1)

        with nc.Block() as blk:
            @blk.gpsimd
            def _(g):
                for i in range(NCHUNK):
                    g.dma_start(out=wa[:, i * 8:(i + 1) * 8],
                                in_=gate_w[i * 128:(i + 1) * 128, :]).then_inc(sm["wl"], 16)
                g.dma_start(out=ba[:], in_=gate_b_row).then_inc(sm["wl"], 16)
                g.dma_start(out=idb[:], in_=ident).then_inc(sm["wl"], 16)

            @blk.sync
            def _(sy):
                for t in range(ntt):
                    p = t % 2
                    if t >= 2:
                        sy.wait_ge(sm["mm"], z["mm"] + 2 * (t - 1))
                    for c in range(NCHUNK):
                        sy.dma_start(out=XT(p, c),
                                     in_=x[c * 128:(c + 1) * 128, t * 128:(t + 1) * 128]
                                     ).then_inc(sm["xl" + str(p)], 16)
                    if t >= 1:
                        tv, pv = t - 1, (t - 1) % 2
                        sy.wait_ge(sm["ev"], z["ev"] + tv + 1)
                        sy.dma_start(out=ohT_out[:, tv * 128:(tv + 1) * 128],
                                     in_=OT_(pv)).then_inc(sm[f"so{pv}"], 16)
                tv, pv = ntt - 1, (ntt - 1) % 2
                sy.wait_ge(sm["ev"], z["ev"] + tv + 1)
                sy.dma_start(out=ohT_out[:, tv * 128:(tv + 1) * 128],
                             in_=OT_(pv)).then_inc(sm[f"so{pv}"], 16)

            @blk.tensor
            def _(te):
                te.wait_ge(sm["wl"], z["wl"] + 16 * (NCHUNK + 2))
                for t in range(ntt):
                    p = t % 2
                    te.wait_ge(sm["xl" + str(p)], xlval(t))
                    if t >= 2:
                        te.wait_ge(sm["oh"], z["oh"] + 2 * (t - 1))
                    for i in range(NCHUNK):
                        te.matmul(psL[p][:], XT(p, i), wa[:, i * 8:(i + 1) * 8],
                                  start=(i == 0), stop=False)
                    te.matmul(psL[p][:], onesr[:], ba[:], start=False, stop=True
                              ).then_inc(sm["mm"], 1)
                    te.wait_ge(sm["oh"], z["oh"] + 2 * t + 2)
                    if t >= 2:
                        te.wait_ge(sm["ev"], z["ev"] + t - 1)
                    te.transpose(psT[p][:], OH(p), idb[:]).then_inc(sm["mm"], 1)

            @blk.vector
            def _(v):
                v.memset(onesr[:], 1.0)
                for t in range(ntt):
                    p = t % 2
                    v.wait_ge(sm["mm"], z["mm"] + 2 * t + 1)
                    v.reduce_max(nm[:, p:p + 1], psL[p][:], AX.X, negate=True
                                 ).then_inc(sm["oh"], 1)
                    v.wait_ge(sm["tp"], z["tp"] + t + 1)
                    v.tensor_scalar(OH(p), SG(p), 1.0, None, ALU.add).then_inc(sm["oh"], 1)
                    v.wait_ge(sm["mm"], z["mm"] + 2 * t + 2)
                    if t >= 2:
                        v.wait_ge(sm[f"so{p}"], z[f"so{p}"] + 16 * ((t - 2) // 2 + 1))
                    v.tensor_copy(OT_(p), psT[p][:]).then_inc(sm["ev"], 1)

            @blk.scalar
            def _(sc):
                for t in range(ntt):
                    p = t % 2
                    sc.wait_ge(sm["oh"], z["oh"] + 2 * t + 1)
                    sc.sign(SG(p), psL[p][:], bias=nm[:, p:p + 1]).then_inc(sm["tp"], 1)

        sm.set_total("wl", z["wl"] + 16 * (NCHUNK + 2))
        for p in range(2):
            nt_p = (ntt + 1 - p) // 2
            sm.set_total("xl" + str(p), z["xl" + str(p)] + 16 * NCHUNK * nt_p)
            sm.set_total(f"so{p}", z[f"so{p}"] + 16 * nt_p)
        sm.set_total("mm", z["mm"] + 2 * ntt)
        sm.set_total("oh", z["oh"] + 2 * ntt)
        sm.set_total("tp", z["tp"] + ntt)
        sm.set_total("ev", z["ev"] + ntt)
        with nc.Block() as blk2:
            @blk2.sync
            def _(sy):
                for p in range(2):
                    sy.wait_ge(sm[f"so{p}"], z[f"so{p}"] + 16 * ((ntt + 1 - p) // 2))
    kb.barrier()


DN4 = float(64.0 ** -0.25)
PRATIO = 0.125
EPS_FEAT = 1e-4
EPS_DEN = 0.01


def emit_perf_qk(kb, *, kind, WT, WP, P, b_row, b_col, x, out, rs_out=None,
                 ident=None, name="pq"):
    nc = kb.nc
    isq = kind == "q"
    sm = SemSet(kb, ["xl0", "xl1"] + [f"st{j}" for j in range(2)] + ["rst"],
                ["pm", "pe", "mq", "sq", "rd", "ex", "ov", "gm"], ["wl"])
    z = sm.z
    ntt = N // 128

    bPb = nc.alloc_sbuf_tensor(name + "_bP", [1, D], BF16)
    with ExitStack() as ctx0:
        E0 = ctx0.enter_context
        pa = E0(nc.sbuf_tensor(name + "_pa", [128, NCHUNK * D], BF16))
        bcol = E0(nc.sbuf_tensor(name + "_bc", [128, NCHUNK], BF16))
        psB = [E0(nc.psum_tensor(name + f"_psB{j}", [1, TT], F32)) for j in range(2)]
        with nc.Block() as blkp:
            @blkp.gpsimd
            def _(g):
                for i in range(NCHUNK):
                    g.dma_start(out=pa[:, i * D:(i + 1) * D],
                                in_=P[i * 128:(i + 1) * 128, :]).then_inc(sm["wl"], 16)
                for i in range(NCHUNK):
                    g.dma_start(out=bcol[:, i:i + 1],
                                in_=b_col[i * 128:(i + 1) * 128, 0:1]).then_inc(sm["wl"], 16)
            @blkp.tensor
            def _(te):
                te.wait_ge(sm["wl"], z["wl"] + 16 * 2 * NCHUNK)
                for sl in range(2):
                    for i in range(NCHUNK):
                        ii = te.matmul(psB[sl][:], bcol[:, i:i + 1],
                                       pa[:, i * D + sl * TT: i * D + sl * TT + TT],
                                       start=(i == 0), stop=(i == NCHUNK - 1))
                    ii.then_inc(sm["pm"], 1)
            @blkp.vector
            def _(v):
                for sl in range(2):
                    v.wait_ge(sm["pm"], z["pm"] + sl + 1)
                    v.tensor_copy(bPb.ap()[0:1, sl * TT:(sl + 1) * TT], psB[sl][:]
                                  ).then_inc(sm["pe"], 1)
        kb.barrier()

    with ExitStack() as ctx:
        E = ctx.enter_context
        wt = E(nc.sbuf_tensor(name + "_wt", [128, NCHUNK * D], BF16))
        wp = E(nc.sbuf_tensor(name + "_wp", [128, NCHUNK * D], BF16))
        ba = E(nc.sbuf_tensor(name + "_ba", [1, D], BF16))
        onesr = E(nc.sbuf_tensor(name + "_or", [1, 128], BF16))
        xa = E(nc.sbuf_tensor(name + "_xa", [128, 2 * NCHUNK * 128], BF16))
        sq = E(nc.sbuf_tensor(name + "_sq", [128, 2 * D], BF16))
        dg = E(nc.sbuf_tensor(name + "_dg", [128, 2 * 16], F32))
        st_ = E(nc.sbuf_tensor(name + "_st", [128, 2 * 16], F32))
        bb = E(nc.sbuf_tensor(name + "_bb", [128, 2 * 16], F32))
        eb = E(nc.sbuf_tensor(name + "_eb", [128, 2 * D], F32))
        ob = E(nc.sbuf_tensor(name + "_ob", [128, 2 * D], BF16))
        stk = E(nc.sbuf_tensor(name + "_sk", [128, 2 * 16], F32)) if not isq else None
        idb = E(nc.sbuf_tensor(name + "_id", [128, 128], F32)) if not isq else None
        rs_sb = E(nc.sbuf_tensor(name + "_rs", [16, 3], F32)) if not isq else None
        rs_row = E(nc.sbuf_tensor(name + "_rr", [1, 16], F32)) if not isq else None
        psQl = [E(nc.psum_tensor(name + f"_psQ{j}", [128, TT], F32)) for j in range(4)]
        psDl = [E(nc.psum_tensor(name + f"_psD{j}", [128, TT], F32)) for j in range(4)]
        PSQ = lambda p, sl: psQl[p * 2 + sl]
        PSD = lambda p, sl: psDl[p * 2 + sl]
        psT = psDl[0] if not isq else None

        XT = lambda p, c: xa[:, (p * NCHUNK + c) * 128:(p * NCHUNK + c) * 128 + 128]
        SQ_ = lambda p, sl: sq[:, (p * 2 + sl) * TT:(p * 2 + sl) * TT + TT]
        DG = lambda p: dg[:, p * 16:(p + 1) * 16]
        ST = lambda p: st_[:, p * 16:(p + 1) * 16]
        BB = lambda p: bb[:, p * 16:(p + 1) * 16]
        STK = lambda j: stk[:, j * 16:(j + 1) * 16] if stk is not None else None
        EB = lambda p, sl: eb[:, (p * 2 + sl) * TT:(p * 2 + sl) * TT + TT]
        OB = lambda p, sl: ob[:, (p * 2 + sl) * TT:(p * 2 + sl) * TT + TT]
        xlval = lambda t: z["xl" + str(t % 2)] + 16 * NCHUNK * (t // 2 + 1)
        stval = lambda t: z["st" + str(t % 2)] + 16 * (t // 2 + 1)
        WLT = 16 * (2 * NCHUNK + 2 * NCHUNK + 1 + (1 if not isq else 0))

        with nc.Block() as blk:
            @blk.gpsimd
            def _(g):
                for i in range(NCHUNK):
                    g.dma_start(out=wt[:, i * D:(i + 1) * D],
                                in_=WT[i * 128:(i + 1) * 128, :]).then_inc(sm["wl"], 16)
                for i in range(NCHUNK):
                    g.dma_start(out=wp[:, i * D:(i + 1) * D],
                                in_=WP[i * 128:(i + 1) * 128, :]).then_inc(sm["wl"], 16)
                g.dma_start(out=ba[:], in_=b_row).then_inc(sm["wl"], 16)
                if not isq:
                    g.dma_start(out=idb[:], in_=ident).then_inc(sm["wl"], 16)

            @blk.sync
            def _(sy):
                for t in range(ntt):
                    p = t % 2
                    if t >= 2:
                        sy.wait_ge(sm["mq"], z["mq"] + t - 1)
                    for c in range(NCHUNK):
                        sy.dma_start(out=XT(p, c),
                                     in_=x[c * 128:(c + 1) * 128, t * 128:(t + 1) * 128]
                                     ).then_inc(sm["xl" + str(p)], 16)
                    if t >= 1:
                        tv, pv = t - 1, (t - 1) % 2
                        sy.wait_ge(sm["ov"], z["ov"] + tv + 1)
                        sy.dma_start(out=out[tv * 128:(tv + 1) * 128, :],
                                     in_=ob[:, pv * D:(pv + 1) * D]).then_inc(sm["st" + str(pv)], 16)
                tv, pv = ntt - 1, (ntt - 1) % 2
                sy.wait_ge(sm["ov"], z["ov"] + tv + 1)
                sy.dma_start(out=out[tv * 128:(tv + 1) * 128, :],
                             in_=ob[:, pv * D:(pv + 1) * D]).then_inc(sm["st" + str(pv)], 16)
                if not isq:
                    sy.wait_ge(sm["gm"], z["gm"] + 32 + 6)
                    sy.dma_start(out=rs_out, in_=rs_row[:]).then_inc(sm["rst"], 16)
                    sy.wait_ge(sm["rst"], z["rst"] + 16)

            @blk.tensor
            def _(te):
                te.wait_ge(sm["wl"], z["wl"] + WLT)
                te.wait_ge(sm["pe"], z["pe"] + 2)
                for t in range(ntt):
                    p = t % 2
                    te.wait_ge(sm["xl" + str(p)], xlval(t))
                    if t >= 2:
                        te.wait_ge(sm["ov"], z["ov"] + t - 1)
                    for sl in range(2):
                        for i in range(NCHUNK):
                            te.matmul(PSQ(p, sl)[:], XT(p, i),
                                      wt[:, i * D + sl * TT: i * D + sl * TT + TT],
                                      start=(i == 0), stop=False)
                        te.matmul(PSQ(p, sl)[:], onesr[:], ba[0:1, sl * TT:(sl + 1) * TT],
                                  start=False, stop=True)
                    for sl in range(2):
                        for i in range(NCHUNK):
                            te.matmul(PSD(p, sl)[:], XT(p, i),
                                      wp[:, i * D + sl * TT: i * D + sl * TT + TT],
                                      start=(i == 0), stop=False)
                        ii = te.matmul(PSD(p, sl)[:], onesr[:], bPb.ap()[0:1, sl * TT:(sl + 1) * TT],
                                       start=False, stop=True)
                    ii.then_inc(sm["mq"], 1)
                if not isq:
                    te.wait_ge(sm["gm"], z["gm"] + 32)
                    te.transpose(psT[0:16, 0:128], STK((ntt - 1) % 2), idb[:]).then_inc(sm["gm"], 1)
                    te.wait_ge(sm["gm"], z["gm"] + 32 + 4)
                    te.transpose(psT[0:1, 0:16], rs_sb[:, 2:3], idb[0:16, 0:16]
                                 ).then_inc(sm["gm"], 1)

            @blk.scalar
            def _(sc):
                for t in range(ntt):
                    p = t % 2
                    sc.wait_ge(sm["mq"], z["mq"] + t + 1)
                    for sl in range(2):
                        ii = sc.activation(SQ_(p, sl), PSQ(p, sl)[:], AF.Square, bias=0.0, scale=0.25)
                    ii.then_inc(sm["sq"], 1)
                    sc.wait_ge(sm["rd"], z["rd"] + t + 1)
                    for h in range(16):
                        sl, hh = h // 8, h % 8
                        ii = sc.activation(EB(p, sl)[:, hh * 64:(hh + 1) * 64],
                                           PSD(p, sl)[:, hh * 64:(hh + 1) * 64], AF.Exp,
                                           bias=BB(p)[:, h:h + 1], scale=DN4)
                    ii.then_inc(sm["ex"], 1)
                if not isq:
                    sc.wait_ge(sm["gm"], z["gm"] + 32 + 2)
                    sc.activation(rs_sb[:, 1:2], rs_sb[:, 0:1], AF.Exp, bias=0.0,
                                  scale=-DN4).then_inc(sm["gm"], 1)

            @blk.vector
            def _(v):
                v.memset(onesr[:], 1.0)
                for t in range(ntt):
                    p = t % 2
                    v.wait_ge(sm["sq"], z["sq"] + t + 1)
                    for sl in range(2):
                        v.tensor_reduce(DG(p)[:, sl * 8:(sl + 1) * 8],
                                        SQ_(p, sl).rearrange("p (h d) -> p h d", h=8),
                                        AX.X, ALU.add)
                        v.tensor_reduce(ST(p)[:, sl * 8:(sl + 1) * 8],
                                        PSD(p, sl)[:].rearrange("p (h d) -> p h d", h=8),
                                        AX.X, ALU.max)
                    if isq:
                        v.scalar_tensor_tensor(BB(p), ST(p), -DN4, DG(p), ALU.mult,
                                               ALU.subtract).then_inc(sm["rd"], 1)
                    else:
                        v.tensor_scalar_mul(BB(p), DG(p), -1.0).then_inc(sm["rd"], 1)
                        if t == 0:
                            v.tensor_copy(STK(0), ST(p))
                        else:
                            v.tensor_tensor(STK(t % 2), STK((t + 1) % 2), ST(p), ALU.max)
                    v.wait_ge(sm["ex"], z["ex"] + t + 1)
                    if t >= 2:
                        v.wait_ge(sm["st" + str(p)], stval(t - 2))
                    for sl in range(2):
                        if isq:
                            ii = v.tensor_scalar(OB(p, sl), EB(p, sl), PRATIO, EPS_FEAT,
                                                 ALU.mult, ALU.add)
                        else:
                            ii = v.tensor_copy(OB(p, sl), EB(p, sl))
                    ii.then_inc(sm["ov"], 1)
                if not isq:
                    v.nop().then_inc(sm["gm"], 32)
                    v.wait_ge(sm["gm"], z["gm"] + 32 + 1)
                    v.reduce_max(rs_sb[:, 0:1], psT[0:16, 0:128], AX.X).then_inc(sm["gm"], 1)
                    v.wait_ge(sm["gm"], z["gm"] + 32 + 3)
                    v.tensor_scalar_mul(rs_sb[:, 2:3], rs_sb[:, 1:2], PRATIO).then_inc(sm["gm"], 1)
                    v.wait_ge(sm["gm"], z["gm"] + 32 + 5)
                    v.tensor_copy(rs_row[:], psT[0:1, 0:16]).then_inc(sm["gm"], 1)

        for p in range(2):
            nt_p = (ntt + 1 - p) // 2
            sm.set_total("xl" + str(p), z["xl" + str(p)] + 16 * NCHUNK * nt_p)
            sm.set_total(f"st{p}", z[f"st{p}"] + 16 * nt_p)
        sm.set_total("wl", z["wl"] + WLT)
        sm.set_total("rst", z["rst"] + (16 if not isq else 0))
        sm.set_total("pm", z["pm"] + 2)
        sm.set_total("pe", z["pe"] + 2)
        for k, n in [("mq", ntt), ("sq", ntt), ("rd", ntt), ("ex", ntt), ("ov", ntt)]:
            sm.set_total(k, z[k] + n)
        sm.set_total("gm", z["gm"] + (0 if isq else 32 + 6))
    kb.barrier()


def emit_perf_kv(kb, *, ek, v_tok, rs_row, kv_out, name="pk2"):
    nc = kb.nc
    sm = SemSet(kb, ["el0", "el1", "vl0", "vl1", "rl", "ko"],
                ["v0", "ve", "mk", "ev", "rr", "fz"])
    z = sm.z
    ntt = N // 128

    v0rep = nc.alloc_sbuf_tensor(name + "_v0r", [64, 16 * 65], F32)
    rsrep = nc.alloc_sbuf_tensor(name + "_rsp", [64, 16], F32)
    with ExitStack() as ctx0:
        E0 = ctx0.enter_context
        va0 = E0(nc.sbuf_tensor(name + "_va0", [128, 4 * TT], BF16))
        onesc = E0(nc.sbuf_tensor(name + "_oc", [128, 1], BF16))
        onesr64 = E0(nc.sbuf_tensor(name + "_o64", [1, 64], F32))
        rsr = E0(nc.sbuf_tensor(name + "_rsr", [1, 16], F32))
        v0e = E0(nc.sbuf_tensor(name + "_v0e", [1, 16 * 65], F32))
        psV = [E0(nc.psum_tensor(name + f"_psV{j}", [1, TT], F32)) for j in range(2)]
        psRp = E0(nc.psum_tensor(name + "_psRp", [64, 16], F32))
        psW = [E0(nc.psum_tensor(name + f"_psW{j}", [64, TT], F32)) for j in range(3)]
        with nc.Block() as blkp:
            @blkp.sync
            def _(sy):
                sy.dma_start(out=rsr[:], in_=rs_row).then_inc(sm["rl"], 16)
                for t in range(ntt):
                    p = t % 2
                    if t >= 2:
                        sy.wait_ge(sm["v0"], z["v0"] + t - 1)
                    for sl in range(2):
                        sy.dma_start(out=va0[:, (p * 2 + sl) * TT:(p * 2 + sl) * TT + TT],
                                     in_=v_tok[t * 128:(t + 1) * 128, sl * TT:(sl + 1) * TT]
                                     ).then_inc(sm["vl" + str(p)], 16)
            @blkp.vector
            def _(v):
                v.memset(onesc[:], 1.0)
                v.memset(onesr64[:], 1.0)
                v.nop().then_inc(sm["ve"], 1)
                v.wait_ge(sm["v0"], z["v0"] + ntt)
                for h in range(16):
                    sl, off = (h * 64) // TT, (h * 64) % TT
                    v.tensor_scalar_mul(v0e[0:1, h * 65:h * 65 + 64],
                                        psV[sl][0:1, off:off + 64], EPS_FEAT)
                    ii = v.memset(v0e[0:1, h * 65 + 64:h * 65 + 65], EPS_FEAT * N)
                ii.then_inc(sm["ve"], 1)
                v.wait_ge(sm["v0"], z["v0"] + ntt + 1)
                v.tensor_copy(rsrep.ap()[:], psRp[:])
                for j in range(3):
                    v.wait_ge(sm["v0"], z["v0"] + ntt + 2 + j)
                    w = min(TT, 16 * 65 - j * TT)
                    ii = v.tensor_copy(v0rep.ap()[:, j * TT:j * TT + w], psW[j][:, 0:w])
                ii.then_inc(sm["ve"], 1)
            @blkp.tensor
            def _(te):
                te.wait_ge(sm["ve"], z["ve"] + 1)
                for t in range(ntt):
                    p = t % 2
                    te.wait_ge(sm["vl" + str(p)], z["vl" + str(p)] + 32 * (t // 2 + 1))
                    for sl in range(2):
                        ii = te.matmul(psV[sl][:], onesc[:],
                                       va0[:, (p * 2 + sl) * TT:(p * 2 + sl) * TT + TT],
                                       start=(t == 0), stop=(t == ntt - 1))
                    ii.then_inc(sm["v0"], 1)
                te.wait_ge(sm["rl"], z["rl"] + 16)
                te.wait_ge(sm["ve"], z["ve"] + 2)
                te.matmul(psRp[:], onesr64[:], rsr[:], start=True, stop=True
                          ).then_inc(sm["v0"], 1)
                for j in range(3):
                    w = min(TT, 16 * 65 - j * TT)
                    te.matmul(psW[j][:, 0:w], onesr64[:], v0e[0:1, j * TT:j * TT + w],
                              start=True, stop=True).then_inc(sm["v0"], 1)
        kb.barrier()

    for p in range(2):
        sm.set_total("vl" + str(p), z["vl" + str(p)] + 32 * ((ntt + 1 - p) // 2))
    sm.set_total("rl", z["rl"] + 16)
    sm.set_total("v0", z["v0"] + ntt + 4)
    sm.set_total("ve", z["ve"] + 3)
    zek = [z["el0"], z["el1"]]

    with ExitStack() as ctx:
        E = ctx.enter_context
        eka = E(nc.sbuf_tensor(name + "_eka", [128, 2 * TT], BF16))
        vaa = E(nc.sbuf_tensor(name + "_vaa", [128, 2 * 8 * 65], BF16))
        t1 = E(nc.sbuf_tensor(name + "_t1", [64, 8 * 65], F32))
        ko = E(nc.sbuf_tensor(name + "_ko", [64, 8 * 65], BF16))
        psK = [E(nc.psum_tensor(name + f"_psK{h}", [64, 65], F32)) for h in range(8)]
        VS = lambda p, h: vaa[:, (p * 8 + h) * 65:(p * 8 + h) * 65 + 65]
        nmk = 0
        with nc.Block() as blk:
            @blk.gpsimd
            def _(g):
                ii = None
                for p in range(2):
                    for h in range(8):
                        ii = g.memset(VS(p, h)[:, 64:65], 1.0)
                ii.then_inc(sm["fz"], 1)

            @blk.sync
            def _(sy):
                for hg in range(2):
                    for t in range(ntt):
                        p = t % 2
                        if t >= 2 or hg >= 1:
                            gidx = hg * ntt + t
                            sy.wait_ge(sm["mk"], z["mk"] + (gidx - 1) * 8)
                        sy.dma_start(out=eka[:, p * TT:(p + 1) * TT],
                                     in_=ek[t * 128:(t + 1) * 128, hg * TT:(hg + 1) * TT]
                                     ).then_inc(sm["el" + str(p)], 16)
                        for h in range(8):
                            sy.dma_start(out=VS(p, h)[:, 0:64],
                                         in_=v_tok[t * 128:(t + 1) * 128,
                                                   (hg * 8 + h) * 64:(hg * 8 + h) * 64 + 64]
                                         ).then_inc(sm["el" + str(p)], 16)
                    for h in range(8):
                        sy.wait_ge(sm["ev"], z["ev"] + hg * 8 + h + 1)
                        sy.dma_start(out=kv_out[:, (hg * 8 + h) * 65:(hg * 8 + h) * 65 + 65],
                                     in_=ko[:, h * 65:(h + 1) * 65]).then_inc(sm["ko"], 16)

            @blk.tensor
            def _(te):
                te.wait_ge(sm["fz"], z["fz"] + 1)
                for hg in range(2):
                    if hg >= 1:
                        te.wait_ge(sm["ev"], z["ev"] + hg * 8)
                    for t in range(ntt):
                        p = t % 2
                        gidx = hg * ntt + t
                        te.wait_ge(sm["el" + str(p)], zek[p] + 16 * 9 * (gidx // 2 + 1))
                        for h in range(8):
                            te.matmul(psK[h][:], eka[:, p * TT + h * 64: p * TT + h * 64 + 64],
                                      VS(p, h), start=(t == 0), stop=(t == ntt - 1))
                        te.nop().then_inc(sm["mk"], 8)


            @blk.vector
            def _(v):
                for hg in range(2):
                    for h in range(8):
                        v.wait_ge(sm["mk"], z["mk"] + (hg + 1) * ntt * 8)
                        if hg >= 1:
                            v.wait_ge(sm["ko"], z["ko"] + 16 * (h + 1))
                        gh = hg * 8 + h
                        v.scalar_tensor_tensor(ko[:, h * 65:(h + 1) * 65], psK[h][:],
                                                rsrep.ap()[:, gh:gh + 1],
                                                v0rep.ap()[:, gh * 65:(gh + 1) * 65],
                                                ALU.mult, ALU.add).then_inc(sm["ev"], 1)

        sm.set_total("mk", z["mk"] + 2 * ntt * 8)
        sm.set_total("ev", z["ev"] + 16)
        sm.set_total("rr", z["rr"])
        sm.set_total("fz", z["fz"] + 1)
        sm.set_total("ko", z["ko"] + 16 * 16)
        for p in range(2):
            sm.set_total("el" + str(p), zek[p] + 16 * 9 * ((2 * ntt + 1 - p) // 2))
        with nc.Block() as blk2:
            @blk2.sync
            def _(sy):
                sy.wait_ge(sm["ko"], z["ko"] + 16 * 16)
    kb.barrier()


def emit_perf_qkv(kb, *, qp, kv, out_y, ident16, name="pq3"):
    nc = kb.nc
    sm = SemSet(kb, ["ql0", "ql1"] + [f"so{j}" for j in range(2)],
                ["tp", "tv", "mm", "fr", "rq", "rv", "fz"], ["kl"])
    z = sm.z
    ntp = N // 128

    with ExitStack() as ctx:
        E = ctx.enter_context
        qa = E(nc.sbuf_tensor(name + "_qa", [128, 2 * 64], BF16))
        qf = E(nc.sbuf_tensor(name + "_qf", [64, N], BF16))
        kva = E(nc.sbuf_tensor(name + "_kv", [64, 16 * 65], BF16))
        idb = E(nc.sbuf_tensor(name + "_id", [128, 128], BF16))
        onesr64 = E(nc.sbuf_tensor(name + "_o64", [1, 64], F32))
        nrow = E(nc.sbuf_tensor(name + "_nr", [1, 4 * TT], F32))
        reps = E(nc.sbuf_tensor(name + "_rp", [64, 2 * TT], F32))
        ot = E(nc.sbuf_tensor(name + "_ot", [64, 2 * TT], BF16))
        psT = [E(nc.psum_tensor(name + f"_psT{j}", [64, 128], BF16)) for j in range(2)]
        psY = [E(nc.psum_tensor(name + f"_psY{j}", [65, TT], F32)) for j in range(2)]
        psR = E(nc.psum_tensor(name + "_psR", [64, TT], F32))

        with nc.Block() as blk:
            @blk.gpsimd
            def _(g):
                g.dma_start(out=idb[:], in_=ident16).then_inc(sm["kl"], 16)
                g.dma_start(out=kva[:], in_=kv).then_inc(sm["kl"], 16)

            @blk.sync
            def _(sy):
                for h in range(NH):
                    for j in range(ntp):
                        gi = h * ntp + j
                        p = gi % 2
                        if gi >= 2:
                            sy.wait_ge(sm["tp"], z["tp"] + gi - 1)
                        sy.dma_start(out=qa[:, p * 64:(p + 1) * 64],
                                     in_=qp[j * 128:(j + 1) * 128, h * 64:(h + 1) * 64]
                                     ).then_inc(sm["ql" + str(p)], 16)
                    for sl in range(NTT):
                        it = h * NTT + sl
                        sy.wait_ge(sm["fz"], z["fz"] + it + 1)
                        sy.dma_start(out=out_y[h * 64:(h + 1) * 64, sl * TT:(sl + 1) * TT],
                                     in_=ot[:, (it % 2) * TT:(it % 2) * TT + TT]
                                     ).then_inc(sm[f"so{it % 2}"], 16)

            @blk.tensor
            def _(te):
                te.wait_ge(sm["kl"], z["kl"] + 32)
                for h in range(NH):
                    for j in range(ntp):
                        gi = h * ntp + j
                        p = gi % 2
                        te.wait_ge(sm["ql" + str(p)], z["ql" + str(p)] + 16 * (gi // 2 + 1))
                        if gi >= 2:
                            te.wait_ge(sm["tv"], z["tv"] + gi - 1)
                        te.transpose(psT[p][:], qa[:, p * 64:(p + 1) * 64], idb[:]
                                     ).then_inc(sm["tp"], 1)
                    for sl in range(NTT):
                        it = h * NTT + sl
                        te.wait_ge(sm["tv"], z["tv"] + h * ntp + (sl + 1) * 4)
                        if it >= 2:
                            te.wait_ge(sm["fz"], z["fz"] + it - 1)
                        te.matmul(psY[it % 2][:], kva[:, h * 65:(h + 1) * 65],
                                  qf[:, sl * TT:(sl + 1) * TT], start=True, stop=True
                                  ).then_inc(sm["mm"], 1)
                        te.wait_ge(sm["fr"], z["fr"] + it + 1)
                        if it >= 1:
                            te.wait_ge(sm["rv"], z["rv"] + it)
                        te.matmul(psR[:], onesr64[:], nrow[0:1, (it % 2) * TT:(it % 2) * TT + TT],
                                  start=True, stop=True).then_inc(sm["rq"], 1)

            @blk.vector
            def _(v):
                v.memset(onesr64[:], 1.0)
                for h in range(NH):
                    for j in range(ntp):
                        gi = h * ntp + j
                        p = gi % 2
                        v.wait_ge(sm["tp"], z["tp"] + gi + 1)
                        if h >= 1:
                            v.wait_ge(sm["mm"], z["mm"] + (h - 1) * NTT + (j // 4 + 1))
                        v.tensor_copy(qf[:, j * 128:(j + 1) * 128], psT[p][:]
                                      ).then_inc(sm["tv"], 1)
                    for sl in range(NTT):
                        it = h * NTT + sl
                        v.wait_ge(sm["mm"], z["mm"] + it + 1)
                        if it >= 2:
                            v.wait_ge(sm["rq"], z["rq"] + it - 1)
                        v.tensor_scalar_max(nrow[0:1, (2 + it % 2) * TT:(2 + it % 2) * TT + TT],
                                            psY[it % 2][64:65, :], EPS_DEN)
                        v.reciprocal(nrow[0:1, (it % 2) * TT:(it % 2) * TT + TT],
                                     nrow[0:1, (2 + it % 2) * TT:(2 + it % 2) * TT + TT]
                                     ).then_inc(sm["fr"], 1)
                        v.wait_ge(sm["rv"], z["rv"] + it + 1)
                        if it >= 2:
                            v.wait_ge(sm[f"so{it % 2}"], z[f"so{it % 2}"] + 16 * ((it - 2) // 2 + 1))
                        v.tensor_tensor(ot[:, (it % 2) * TT:(it % 2) * TT + TT],
                                        psY[it % 2][0:64, :],
                                        reps[:, (it % 2) * TT:(it % 2) * TT + TT],
                                        ALU.mult).then_inc(sm["fz"], 1)

            @blk.scalar
            def _(sc):
                for h in range(NH):
                    for sl in range(NTT):
                        it = h * NTT + sl
                        sc.wait_ge(sm["rq"], z["rq"] + it + 1)
                        if it >= 2:
                            sc.wait_ge(sm["fz"], z["fz"] + it - 1)
                        sc.copy(reps[:, (it % 2) * TT:(it % 2) * TT + TT], psR[:]
                                ).then_inc(sm["rv"], 1)

        tot = NH * NTT
        for p in range(2):
            sm.set_total("ql" + str(p), z["ql" + str(p)] + 16 * (NH * ntp // 2))
            sm.set_total(f"so{p}", z[f"so{p}"] + 16 * (tot // 2))
        sm.set_total("kl", z["kl"] + 32)
        sm.set_total("tp", z["tp"] + NH * ntp)
        sm.set_total("tv", z["tv"] + NH * ntp)
        for k in ["mm", "fr", "rq", "rv", "fz"]:
            sm.set_total(k, z[k] + tot)
        with nc.Block() as blk2:
            @blk2.sync
            def _(sy):
                for p in range(2):
                    sy.wait_ge(sm[f"so{p}"], z[f"so{p}"] + 16 * (tot // 2))
    kb.barrier()


def emit_rope(kb, *, src, out, tab1, tab2, qs2_col, sgn_col, name="rp"):
    nc = kb.nc
    sm = SemSet(kb, ["xl0", "xl1", "st0", "st1", "gs0", "gs1"],
                ["tf", "av", "ag"], ["tl", "gl0", "gl1"])
    z = sm.z

    with ExitStack() as ctx:
        E = ctx.enter_context
        t1r = E(nc.sbuf_tensor(name + "_t1", [128, N], BF16))
        t2r = E(nc.sbuf_tensor(name + "_t2", [128, N], BF16))
        e1 = E(nc.sbuf_tensor(name + "_e1", [128, N], BF16))
        e2 = E(nc.sbuf_tensor(name + "_e2", [128, N], BF16))
        qs = E(nc.sbuf_tensor(name + "_qs", [128, 2], F32))
        sg = E(nc.sbuf_tensor(name + "_sg", [128, 1], F32))
        xv = E(nc.sbuf_tensor(name + "_xv", [128, 4 * N], BF16))
        xg = E(nc.sbuf_tensor(name + "_xg", [128, 4 * N], BF16))
        ov = E(nc.sbuf_tensor(name + "_ov", [128, 2 * N], BF16))
        og = E(nc.sbuf_tensor(name + "_og", [128, 2 * N], BF16))
        tv = E(nc.sbuf_tensor(name + "_tv", [128, 2 * N], BF16))
        tg = E(nc.sbuf_tensor(name + "_tg", [128, 2 * N], BF16))
        XV = lambda p, j: xv[:, (p * 2 + j) * N:(p * 2 + j) * N + N]
        XG = lambda p, j: xg[:, (p * 2 + j) * N:(p * 2 + j) * N + N]
        OV = lambda p: ov[:, p * N:(p + 1) * N]
        OG = lambda p: og[:, p * N:(p + 1) * N]
        TV = lambda p: tv[:, p * N:(p + 1) * N]
        TG = lambda p: tg[:, p * N:(p + 1) * N]

        def ldswap(eng, dst, c, s):
            base = c * 128
            for b4, srow in [(0, 32), (1, 0), (2, 96), (3, 64)]:
                eng.dma_start(out=dst[b4 * 32:(b4 + 1) * 32, :],
                              in_=src[base + srow:base + srow + 32, :]).then_inc(s, 16)

        with nc.Block() as blk:
            @blk.gpsimd
            def _(g):
                for b4 in range(4):
                    g.dma_start(out=t1r[b4 * 32:(b4 + 1) * 32, :], in_=tab1).then_inc(sm["tl"], 16)
                    g.dma_start(out=t2r[b4 * 32:(b4 + 1) * 32, :], in_=tab2).then_inc(sm["tl"], 16)
                g.dma_start(out=qs[:, 0:1], in_=qs2_col).then_inc(sm["tl"], 16)
                for b4, srow in [(0, 32), (1, 0), (2, 96), (3, 64)]:
                    g.dma_start(out=qs[b4 * 32:(b4 + 1) * 32, 1:2],
                                in_=qs2_col[srow:srow + 32, 0:1]).then_inc(sm["tl"], 16)
                g.dma_start(out=sg[:], in_=sgn_col).then_inc(sm["tl"], 16)
                for j in range(4):
                    c = 4 + j
                    p = j % 2
                    if j >= 2:
                        g.wait_ge(sm["ag"], z["ag"] + j - 1)
                    g.dma_start(out=XG(p, 0), in_=src[c * 128:(c + 1) * 128, :]
                                ).then_inc(sm["gl" + str(p)], 16)
                    ldswap(g, XG(p, 1), c, sm["gl" + str(p)])
                    g.wait_ge(sm["tf"], z["tf"] + 1)
                    g.wait_ge(sm["gl" + str(p)], z["gl" + str(p)] + 16 * 5 * (j // 2 + 1))
                    if j >= 2:
                        g.wait_ge(sm["gs" + str(p)], z["gs" + str(p)] + 16 * (j // 2))
                    g.tensor_tensor(TG(p), XG(p, 0), e1[:], ALU.mult)
                    g.tensor_tensor(OG(p), XG(p, 1), e2[:], ALU.mult)
                    g.tensor_tensor(OG(p), OG(p), TG(p), ALU.add).then_inc(sm["ag"], 1)

            @blk.vector
            def _(v):
                v.wait_ge(sm["tl"], z["tl"] + 16 * 14)
                v.tensor_single_scalar(e1[:], t1r[:], qs[:, 0:1], ALU.mult)
                v.tensor_scalar(e2[:], t2r[:], qs[:, 1:2], sg[:, 0:1], ALU.mult, ALU.mult
                                ).then_inc(sm["tf"], 1)
                for j in range(4):
                    p = j % 2
                    v.wait_ge(sm["xl" + str(p)], z["xl" + str(p)] + 16 * 5 * (j // 2 + 1))
                    if j >= 2:
                        v.wait_ge(sm["st" + str(p)], z["st" + str(p)] + 16 * (j // 2))
                    v.tensor_tensor(TV(p), XV(p, 0), e1[:], ALU.mult)
                    v.tensor_tensor(OV(p), XV(p, 1), e2[:], ALU.mult)
                    v.tensor_tensor(OV(p), OV(p), TV(p), ALU.add).then_inc(sm["av"], 1)

            @blk.sync
            def _(sy):
                for j in range(4):
                    p = j % 2
                    if j >= 2:
                        sy.wait_ge(sm["av"], z["av"] + j - 1)
                    sy.dma_start(out=XV(p, 0), in_=src[j * 128:(j + 1) * 128, :]
                                 ).then_inc(sm["xl" + str(p)], 16)
                    ldswap(sy, XV(p, 1), j, sm["xl" + str(p)])
                    if j >= 1:
                        pv = (j - 1) % 2
                        sy.wait_ge(sm["av"], z["av"] + j)
                        sy.dma_start(out=out[(j - 1) * 128:j * 128, :], in_=OV(pv)
                                     ).then_inc(sm["st" + str(pv)], 16)
                        sy.wait_ge(sm["ag"], z["ag"] + j)
                        sy.dma_start(out=out[(4 + j - 1) * 128:(4 + j) * 128, :], in_=OG(pv)
                                     ).then_inc(sm["gs" + str(pv)], 16)
                sy.wait_ge(sm["av"], z["av"] + 4)
                sy.dma_start(out=out[3 * 128:4 * 128, :], in_=OV(1)).then_inc(sm["st1"], 16)
                sy.wait_ge(sm["ag"], z["ag"] + 4)
                sy.dma_start(out=out[7 * 128:8 * 128, :], in_=OG(1)).then_inc(sm["gs1"], 16)
                for p in range(2):
                    sy.wait_ge(sm["st" + str(p)], z["st" + str(p)] + 32)
                    sy.wait_ge(sm["gs" + str(p)], z["gs" + str(p)] + 32)

        for p in range(2):
            sm.set_total("xl" + str(p), z["xl" + str(p)] + 16 * 5 * 2)
            sm.set_total("gl" + str(p), z["gl" + str(p)] + 16 * 5 * 2)
            sm.set_total("st" + str(p), z["st" + str(p)] + 32)
            sm.set_total("gs" + str(p), z["gs" + str(p)] + 32)
        sm.set_total("tl", z["tl"] + 16 * 14)
        sm.set_total("tf", z["tf"] + 1)
        sm.set_total("av", z["av"] + 4)
        sm.set_total("ag", z["ag"] + 4)
    kb.barrier()


def emit_l2norms(kb, *, qf, kf, qinv_out, kinv_out, bd16, kivT_out=None,
                 ident=None, eps=1e-12, name="l2"):
    nc = kb.nc
    has_t = kivT_out is not None
    sm = SemSet(kb, ["ld0", "ld1", "sq0", "sq1"], ["qs", "mm", "rw", "tk", "kc"],
                ["cl", "ks"])
    z = sm.z
    ntt = NTT

    with ExitStack() as ctx:
        E = ctx.enter_context
        xa = E(nc.sbuf_tensor(name + "_xa", [128, 2 * NCHUNK * TT], BF16))
        sq = E(nc.sbuf_tensor(name + "_sq", [128, 2 * NCHUNK * TT], BF16))
        bd = E(nc.sbuf_tensor(name + "_bd", [128, NCHUNK * 16], BF16))
        rws = E(nc.sbuf_tensor(name + "_rw", [16, 2 * 3 * TT], F32))
        idb = E(nc.sbuf_tensor(name + "_id", [128, 128], F32)) if has_t else None
        ktc = E(nc.sbuf_tensor(name + "_kt", [128, 32], F32)) if has_t else None
        ps_s = [E(nc.psum_tensor(name + f"_pss{p}", [16, TT], F32)) for p in range(2)]
        psK = [E(nc.psum_tensor(name + f"_psk{p}", [128, 16], F32)) for p in range(2)] if has_t else None
        XT = lambda p, c: xa[:, (p * NCHUNK + c) * TT:(p * NCHUNK + c) * TT + TT]
        SQT = lambda p, c: sq[:, (p * NCHUNK + c) * TT:(p * NCHUNK + c) * TT + TT]
        BD = lambda c: bd[:, c * 16:(c + 1) * 16]
        RW = lambda p, j: rws[:, (p * 3 + j) * TT:(p * 3 + j) * TT + TT]
        RWS = lambda p, s: rws[:, (p * 3 + 2) * TT + s * 128:(p * 3 + 2) * TT + s * 128 + 128]
        KTC = lambda idx: ktc[:, (idx % 2) * 16:(idx % 2) * 16 + 16]

        with nc.Block() as blk:
            @blk.gpsimd
            def _(g):
                for c in range(NCHUNK):
                    g.dma_start(out=BD(c), in_=bd16[c * 128:(c + 1) * 128, :]).then_inc(sm["cl"], 16)
                if has_t:
                    g.dma_start(out=idb[:], in_=ident).then_inc(sm["cl"], 16)
                    for idx in range(32):
                        g.wait_ge(sm["kc"], z["kc"] + idx + 1)
                        g.dma_start(out=kivT_out[:, idx * 16:(idx + 1) * 16],
                                    in_=KTC(idx)).then_inc(sm["ks"], 16)
                    g.wait_ge(sm["ks"], z["ks"] + 16 * 32)

            def store_j(sy, jv):
                pv = jv % 2
                dst = qinv_out if jv < ntt else kinv_out
                tv = jv % ntt
                sy.wait_ge(sm["rw"], z["rw"] + jv + 1)
                sy.dma_start(out=dst[:, tv * TT:(tv + 1) * TT],
                             in_=RW(pv, 2)).then_inc(sm["sq" + str(pv)], 16)

            @blk.sync
            def _(sy):
                for j in range(2 * ntt):
                    srcx = qf if j < ntt else kf
                    t = j % ntt
                    p = j % 2
                    if j >= 2:
                        sy.wait_ge(sm["mm"], z["mm"] + j - 1)
                    for c in range(NCHUNK):
                        sy.dma_start(out=XT(p, c),
                                     in_=srcx[c * 128:(c + 1) * 128, t * TT:(t + 1) * TT]
                                     ).then_inc(sm["ld" + str(p)], 16)
                    if j >= 1:
                        store_j(sy, j - 1)
                store_j(sy, 2 * ntt - 1)
                for p in range(2):
                    sy.wait_ge(sm["sq" + str(p)], z["sq" + str(p)] + 16 * ntt)

            @blk.scalar
            def _(sc):
                for j in range(2 * ntt):
                    p = j % 2
                    sc.wait_ge(sm["ld" + str(p)], z["ld" + str(p)] + 16 * NCHUNK * (j // 2 + 1))
                    if j >= 2:
                        sc.wait_ge(sm["mm"], z["mm"] + j - 1)
                    for c in range(NCHUNK):
                        ii = sc.activation(SQT(p, c), XT(p, c), AF.Square, bias=0.0, scale=1.0)
                    ii.then_inc(sm["qs"], 1)
                    sc.wait_ge(sm["mm"], z["mm"] + j + 1)
                    if j >= 2:
                        sc.wait_ge(sm["rw"], z["rw"] + j - 1)
                    sc.activation(RW(p, 0), ps_s[p][:], AF.Sqrt, bias=0.0, scale=1.0
                                  ).then_inc(sm["qs"], 1)

            @blk.tensor
            def _(te):
                te.wait_ge(sm["cl"], z["cl"] + 16 * (NCHUNK + (1 if has_t else 0)))
                for j in range(2 * ntt):
                    p = j % 2
                    te.wait_ge(sm["qs"], z["qs"] + 2 * j + 1)
                    if j >= 2:
                        te.wait_ge(sm["rw"], z["rw"] + j - 1)
                    for c in range(NCHUNK):
                        ii = te.matmul(ps_s[p][:], BD(c), SQT(p, c), start=(c == 0),
                                       stop=(c == NCHUNK - 1))
                    ii.then_inc(sm["mm"], 1)
                    if has_t and j >= ntt:
                        jj = j - ntt
                        te.wait_ge(sm["rw"], z["rw"] + j + 1)
                        for s in range(4):
                            idx = jj * 4 + s
                            if idx >= 2:
                                te.wait_ge(sm["kc"], z["kc"] + idx - 1)
                            te.transpose(psK[idx % 2][:], RWS(p, s), idb[0:16, 0:16]
                                         ).then_inc(sm["tk"], 1)

            @blk.vector
            def _(v):
                for j in range(2 * ntt):
                    p = j % 2
                    v.wait_ge(sm["qs"], z["qs"] + 2 * j + 2)
                    if j >= 2:
                        v.wait_ge(sm["sq" + str(p)], z["sq" + str(p)] + 16 * ((j - 2) // 2 + 1))
                    if has_t and j >= ntt + 2:
                        v.wait_ge(sm["tk"], z["tk"] + 4 * (j - ntt - 1))
                    v.tensor_scalar_max(RW(p, 1), RW(p, 0), float(eps))
                    v.reciprocal(RW(p, 2), RW(p, 1)).then_inc(sm["rw"], 1)
                    if has_t and j >= ntt:
                        jj = j - ntt
                        for s in range(4):
                            idx = jj * 4 + s
                            v.wait_ge(sm["tk"], z["tk"] + idx + 1)
                            if idx >= 2:
                                v.wait_ge(sm["ks"], z["ks"] + 16 * (idx - 1))
                            v.tensor_copy(KTC(idx), psK[idx % 2][:]).then_inc(sm["kc"], 1)

        for p in range(2):
            sm.set_total("ld" + str(p), z["ld" + str(p)] + 16 * NCHUNK * ntt)
            sm.set_total("sq" + str(p), z["sq" + str(p)] + 16 * ntt)
        sm.set_total("cl", z["cl"] + 16 * (NCHUNK + (1 if has_t else 0)))
        sm.set_total("qs", z["qs"] + 4 * ntt)
        sm.set_total("mm", z["mm"] + 2 * ntt)
        sm.set_total("rw", z["rw"] + 2 * ntt)
        sm.set_total("tk", z["tk"] + (32 if has_t else 0))
        sm.set_total("kc", z["kc"] + (32 if has_t else 0))
        sm.set_total("ks", z["ks"] + (16 * 32 if has_t else 0))
    kb.barrier()


def emit_lstab(kb, *, qs2_6, ks2_6, ident, out, name="ls"):
    nc = kb.nc
    nl = len(qs2_6)
    sm = SemSet(kb, ["ld"], ["tp", "rd", "bc"], ["st"])
    z = sm.z
    with ExitStack() as ctx:
        E = ctx.enter_context
        qsk = E(nc.sbuf_tensor(name + "_qk", [128, 2 * nl], F32))
        idb = E(nc.sbuf_tensor(name + "_id", [128, 128], F32))
        sb = E(nc.sbuf_tensor(name + "_sb", [1, 4 * nl], F32))
        ob = E(nc.sbuf_tensor(name + "_ob", [128, nl], F32))
        onesr = E(nc.sbuf_tensor(name + "_or", [1, 128], F32))
        psT = E(nc.psum_tensor(name + "_pt", [1, 256], F32))
        psB = E(nc.psum_tensor(name + "_pb", [128, nl], F32))
        with nc.Block() as blk:
            @blk.scalar
            def _(sc):
                sc.dma_start(out=idb[:], in_=ident).then_inc(sm["ld"], 16)
                for i in range(nl):
                    sc.dma_start(out=qsk[:, 2 * i:2 * i + 1], in_=qs2_6[i]).then_inc(sm["ld"], 16)
                    sc.dma_start(out=qsk[:, 2 * i + 1:2 * i + 2], in_=ks2_6[i]).then_inc(sm["ld"], 16)

            @blk.tensor
            def _(te):
                te.wait_ge(sm["ld"], z["ld"] + 16 * (2 * nl + 1))
                for i in range(nl):
                    if i >= 1:
                        te.wait_ge(sm["rd"], z["rd"] + 2 * i)
                    te.transpose(psT[0:1, 0:128], qsk[:, 2 * i:2 * i + 1], idb[:])
                    te.transpose(psT[0:1, 128:256], qsk[:, 2 * i + 1:2 * i + 2], idb[:]
                                 ).then_inc(sm["tp"], 1)
                te.wait_ge(sm["rd"], z["rd"] + 2 * nl + 1)
                te.matmul(psB[:], onesr[:], sb[0:1, 3 * nl:4 * nl], start=True,
                          stop=True).then_inc(sm["bc"], 1)

            @blk.vector
            def _(v):
                v.memset(onesr[:], 1.0)
                for i in range(nl):
                    v.wait_ge(sm["tp"], z["tp"] + i + 1)
                    v.reduce_max(sb[0:1, 2 * i:2 * i + 1], psT[0:1, 0:128], AX.X,
                                 apply_absolute_value=True)
                    v.reduce_max(sb[0:1, 2 * i + 1:2 * i + 2], psT[0:1, 128:256], AX.X,
                                 apply_absolute_value=True).then_inc(sm["rd"], 2)
                ii = None
                for i in range(nl):
                    v.tensor_tensor(sb[0:1, 2 * nl + i:2 * nl + i + 1],
                                    sb[0:1, 2 * i:2 * i + 1],
                                    sb[0:1, 2 * i + 1:2 * i + 2], ALU.mult)
                    ii = v.tensor_scalar(sb[0:1, 3 * nl + i:3 * nl + i + 1],
                                         sb[0:1, 2 * nl + i:2 * nl + i + 1],
                                         -8.0, -2.0, ALU.mult, ALU.add)
                ii.then_inc(sm["rd"], 1)
                v.wait_ge(sm["bc"], z["bc"] + 1)
                v.tensor_copy(ob[:], psB[:]).then_inc(sm["rd"], 1)

            @blk.gpsimd
            def _(g):
                g.wait_ge(sm["rd"], z["rd"] + 2 * nl + 2)
                g.dma_start(out=out, in_=ob[:]).then_inc(sm["st"], 16)
                g.wait_ge(sm["st"], z["st"] + 16)

        sm.set_total("ld", z["ld"] + 16 * (2 * nl + 1))
        sm.set_total("tp", z["tp"] + nl)
        sm.set_total("rd", z["rd"] + 2 * nl + 2)
        sm.set_total("bc", z["bc"] + 1)
        sm.set_total("st", z["st"] + 16)
    kb.barrier()


def emit_lattn(kb, *, qop, kop, v_tok, qinv, kivT, ngc_src, lidx, bd2T,
               out_y, den_out, masks, name="la"):
    nc = kb.nc
    NQT = 8
    sm = SemSet(kb, ["ql", "ncl", "kvl", "sd"] + [f"so{j}" for j in range(4)],
                ["ms", "ex", "mka", "mv", "fz", "clm",
                 "qh", "qr"], ["cl", "dl"])
    z = sm.z

    def band(qt):
        return list(range(max(0, 4 * qt - 2), 4 * qt + 4))

    seq = [(hp, qt) for hp in range(8) for qt in range(NQT)]
    cum_kt = {}
    nk = {}
    tot_kt = 0
    for (hp, qt) in seq:
        cum_kt[(hp, qt)] = tot_kt
        nk[(hp, qt)] = len(band(qt))
        tot_kt += nk[(hp, qt)]
    n_items = 2
    item_of = lambda hp, qt: (hp * NQT + qt) * n_items
    tot_items = 8 * NQT * n_items

    with ExitStack() as ctx:
        E = ctx.enter_context
        qa = E(nc.sbuf_tensor(name + "_qa", [128, N], BF16))
        ka = E(nc.sbuf_tensor(name + "_ka", [128, N], BF16))
        va = E(nc.sbuf_tensor(name + "_va", [128, 32 * 130], BF16))
        qh = E(nc.sbuf_tensor(name + "_qh", [128, 2 * TT], BF16))
        qiv2 = E(nc.sbuf_tensor(name + "_qi", [2, 4 * TT], BF16))
        kvc = E(nc.sbuf_tensor(name + "_kv", [128, 512], F32))
        pb = E(nc.sbuf_tensor(name + "_pb", [128, 4 * TT], BF16))
        smt = E(nc.sbuf_tensor(name + "_sm", [128, 4 * TT], F32))
        ngc = E(nc.sbuf_tensor(name + "_ng", [128, 6], F32))
        mka_ = E(nc.sbuf_tensor(name + "_mk", [128, 6 * TT], F32))
        bd2 = E(nc.sbuf_tensor(name + "_b2", [2, 128], BF16))
        dnb = E(nc.sbuf_tensor(name + "_dn", [1, 4 * TT], F32))
        ot = E(nc.sbuf_tensor(name + "_ot", [64, 4 * TT], BF16))
        psS = [[E(nc.psum_tensor(name + f"_psS{h}{j}", [128, TT], F32)) for j in range(2)]
               for h in range(2)]
        psY = [E(nc.psum_tensor(name + f"_psY{h}", [65, TT], F32)) for h in range(2)]
        psQ = [E(nc.psum_tensor(name + f"_psQ{j}", [128, TT], F32)) for j in range(2)]

        n_pairs = tot_items // 2
        PB = lambda h, i: pb[:, (h * 2 + i % 2) * TT:(h * 2 + i % 2) * TT + TT]
        SMT = lambda h, i: smt[:, (h * 2 + i % 2) * TT:(h * 2 + i % 2) * TT + TT]
        VA = lambda h, kt: va[:, kt * 130 + h * 65: kt * 130 + h * 65 + 65]
        MK = lambda j: mka_[:, j * TT:(j + 1) * TT]
        OTT = lambda it: ot[:, (it % 4) * TT:(it % 4) * TT + TT]
        DNB = lambda it: dnb[0:1, (it % 4) * TT:(it % 4) * TT + TT]
        QI2 = lambda pr: qiv2[0:2, (pr % 4) * TT:(pr % 4) * TT + TT]
        KVC = lambda hp, h, kt: kvc[:, kt * 16 + hp * 2 + h: kt * 16 + hp * 2 + h + 1]
        QHS = lambda h, qt: qh[h * 64:(h + 1) * 64, (qt % 2) * TT:(qt % 2) * TT + TT]

        with nc.Block() as blk:
            @blk.gpsimd
            def _(g):
                for j in range(6):
                    g.dma_start(out=MK(j), in_=masks[j * 128:(j + 1) * 128, :]
                                ).then_inc(sm["cl"], 16)
                g.dma_start(out=kvc[:], in_=kivT).then_inc(sm["kvl"], 16)
                ii = None
                for kt in range(32):
                    for h in range(2):
                        ii = g.memset(va[:, kt * 130 + h * 65 + 64: kt * 130 + h * 65 + 65], 1.0)
                ii.then_inc(sm["clm"], 1)
                for (hp, qt) in seq:
                    pr = hp * NQT + qt
                    if pr >= 4:
                        g.wait_ge(sm["qh"], z["qh"] + 2 * (pr - 3))
                    g.dma_start(out=QI2(pr),
                                in_=qinv[hp * 2:hp * 2 + 2,
                                         qt * TT:(qt + 1) * TT]).then_inc(sm["dl"], 16)

            @blk.scalar
            def _(sc):
                sc.dma_start(out=ngc[:], in_=ngc_src).then_inc(sm["ncl"], 16)
                sc.dma_start(out=bd2[:], in_=bd2T).then_inc(sm["ncl"], 16)
                sc.wait_ge(sm["ncl"], z["ncl"] + 32)
                for (hp, qt) in seq:
                    ck = cum_kt[(hp, qt)]
                    base = item_of(hp, qt)
                    pr = hp * NQT + qt
                    for i, kt in enumerate(band(qt)):
                        for h in range(2):
                            sc.wait_ge(sm["mka"], z["mka"] + 2 * (ck + i) + 1 + h)
                            if ck + i >= 2:
                                sc.wait_ge(sm["mv"], z["mv"] + ck + i - 1)
                            sc.activation(PB(h, i), SMT(h, i), AF.Exp,
                                          bias=ngc[:, lidx:lidx + 1], scale=8.0
                                          ).then_inc(sm["ex"], 1)

            @blk.tensor
            def _(te):
                te.wait_ge(sm["ncl"], z["ncl"] + 32)
                te.wait_ge(sm["clm"], z["clm"] + 1)
                for (hp, qt) in seq:
                    ck = cum_kt[(hp, qt)]
                    base = item_of(hp, qt)
                    pr = hp * NQT + qt
                    if qt == 0:
                        te.wait_ge(sm["ql"], z["ql"] + 16 * 80 * (hp + 1))
                    te.wait_ge(sm["dl"], z["dl"] + 16 * (pr + 1))
                    if pr >= 2:
                        te.wait_ge(sm["qh"], z["qh"] + 2 * (pr - 1))
                    te.matmul(psQ[pr % 2][:], bd2[:], QI2(pr), start=True,
                              stop=True).then_inc(sm["qr"], 1)
                    for i, kt in enumerate(band(qt)):
                        for h in range(2):
                            if ck + i >= 2:
                                te.wait_ge(sm["mka"], z["mka"] + 2 * (ck + i - 2) + 1 + h)
                            te.wait_ge(sm["qh"], z["qh"] + base + h + 1)
                            te.matmul(psS[h][i % 2][:],
                                      ka[h * 64:(h + 1) * 64, kt * 128:(kt + 1) * 128],
                                      QHS(h, qt),
                                      start=True, stop=True,
                                      tile_position=(h * 64, 0)).then_inc(sm["ms"], 1)
                        if base >= n_items and i == 0:
                            te.wait_ge(sm["fz"], z["fz"] + base)
                        first = i == 0
                        last = i == nk[(hp, qt)] - 1
                        for h in range(2):
                            te.wait_ge(sm["ex"], z["ex"] + 2 * (ck + i) + 1 + h)
                            ii = te.matmul(psY[h][:], VA(h, kt), PB(h, i),
                                           start=first, stop=last)
                        ii.then_inc(sm["mv"], 1)

            @blk.vector
            def _(v):
                v.wait_ge(sm["kvl"], z["kvl"] + 16)
                for (hp, qt) in seq:
                    ck = cum_kt[(hp, qt)]
                    base = item_of(hp, qt)
                    pr = hp * NQT + qt
                    v.wait_ge(sm["qr"], z["qr"] + pr + 1)
                    if ck >= 1:
                        v.wait_ge(sm["ms"], z["ms"] + 2 * ck)
                    for h in range(2):
                        v.tensor_tensor(QHS(h, qt),
                                        qa[h * 64:(h + 1) * 64, qt * TT:(qt + 1) * TT],
                                        psQ[pr % 2][h * 64:(h + 1) * 64, :], ALU.mult
                                        ).then_inc(sm["qh"], 1)
                    for i, kt in enumerate(band(qt)):
                        for h in range(2):
                            v.wait_ge(sm["ms"], z["ms"] + 2 * (ck + i) + 1 + h)
                            if ck + i >= 2:
                                v.wait_ge(sm["ex"], z["ex"] + 2 * (ck + i - 2) + 1 + h)
                            v.scalar_tensor_tensor(SMT(h, i), psS[h][i % 2][:],
                                                   KVC(hp, h, kt),
                                                   MK(kt - 4 * qt + 2), ALU.mult, ALU.add
                                                   ).then_inc(sm["mka"], 1)
                    for it in range(n_items):
                        h = it
                        gi = base + it
                        v.wait_ge(sm["mv"], z["mv"] + ck + nk[(hp, qt)])
                        if gi >= 4:
                            v.wait_ge(sm[f"so{gi % 4}"], z[f"so{gi % 4}"]
                                      + 16 * ((gi - 4) // 4 + 1))
                            v.wait_ge(sm["sd"], z["sd"] + 16 * (gi - 3))
                        v.tensor_copy(OTT(gi), psY[h][0:64, :])
                        v.tensor_copy(DNB(gi), psY[h][64:65, :]).then_inc(sm["fz"], 1)

            @blk.sync
            def _(sy):
                for hp in range(8):
                    if hp >= 1:
                        sy.wait_ge(sm["mv"], z["mv"] + cum_kt[(hp, 0)])
                        sy.wait_ge(sm["fz"], z["fz"] + item_of(hp, 0))
                    for c in range(NCHUNK):
                        sy.dma_start(out=qa[:, c * TT:(c + 1) * TT],
                                     in_=qop[hp * 128:(hp + 1) * 128, c * TT:(c + 1) * TT]
                                     ).then_inc(sm["ql"], 16)
                        sy.dma_start(out=ka[:, c * TT:(c + 1) * TT],
                                     in_=kop[hp * 128:(hp + 1) * 128, c * TT:(c + 1) * TT]
                                     ).then_inc(sm["ql"], 16)
                    for kt in range(32):
                        for h in range(2):
                            sy.dma_start(out=VA(h, kt)[:, 0:64],
                                         in_=v_tok[kt * 128:(kt + 1) * 128,
                                                   (hp * 2 + h) * 64:(hp * 2 + h) * 64 + 64]
                                         ).then_inc(sm["ql"], 16)
                    for qt in range(NQT):
                        base = item_of(hp, qt)
                        for it in range(n_items):
                            h = it
                            gi = base + it
                            sy.wait_ge(sm["fz"], z["fz"] + gi + 1)
                            sy.dma_start(out=out_y[(hp * 2 + h) * 64:(hp * 2 + h) * 64 + 64,
                                                   qt * TT:(qt + 1) * TT],
                                         in_=OTT(gi)).then_inc(sm[f"so{gi % 4}"], 16)
                            sy.dma_start(out=den_out[hp * 2 + h:hp * 2 + h + 1,
                                                     qt * TT:(qt + 1) * TT],
                                         in_=DNB(gi)).then_inc(sm["sd"], 16)

        sm.set_total("ql", z["ql"] + 16 * 80 * 8)
        sm.set_total("ncl", z["ncl"] + 32)
        sm.set_total("kvl", z["kvl"] + 16)
        sm.set_total("dl", z["dl"] + 16 * n_pairs)
        sm.set_total("sd", z["sd"] + 16 * tot_items)
        sm.set_total("cl", z["cl"] + 16 * 6)
        sm.set_total("clm", z["clm"] + 1)
        for k, n in [("ms", 2 * tot_kt), ("ex", 2 * tot_kt), ("mka", 2 * tot_kt),
                     ("mv", tot_kt), ("qh", tot_items), ("qr", n_pairs),
                     ("fz", tot_items)]:
            sm.set_total(k, z[k] + n)
        for j in range(4):
            sm.set_total(f"so{j}", z[f"so{j}"] + 16 * (tot_items // 4))
        with nc.Block() as blk2:
            @blk2.sync
            def _(sy):
                for j in range(4):
                    sy.wait_ge(sm[f"so{j}"], z[f"so{j}"] + 16 * (tot_items // 4))
                sy.wait_ge(sm["sd"], z["sd"] + 16 * tot_items)
    kb.barrier()


def emit_lnorm(kb, *, num, den, out, bd33b, name="lno"):
    nc = kb.nc
    sm = SemSet(kb, ["nl", "bl", "dn"] + [f"so{j}" for j in range(4)],
                ["rc", "rq", "mu"], [])
    z = sm.z
    NPT = 64
    with ExitStack() as ctx:
        E = ctx.enter_context
        na = E(nc.sbuf_tensor(name + "_na", [128, 4 * TT], BF16))
        db = E(nc.sbuf_tensor(name + "_db", [33, 4 * TT], F32))
        rc3 = E(nc.sbuf_tensor(name + "_rc", [33, 4 * TT], BF16))
        bd33 = E(nc.sbuf_tensor(name + "_b3", [33, 128], BF16))
        ot = E(nc.sbuf_tensor(name + "_ot", [128, 4 * TT], BF16))
        psR = [E(nc.psum_tensor(name + f"_psR{j}", [128, TT], F32)) for j in range(2)]
        NA = lambda pt: na[:, (pt % 4) * TT:(pt % 4) * TT + TT]
        DB = lambda h, pt: db[h * 32:h * 32 + 1, (pt % 4) * TT:(pt % 4) * TT + TT]
        RC = lambda pt: rc3[0:33, (pt % 4) * TT:(pt % 4) * TT + TT]
        RCH = lambda h, pt: rc3[h * 32:h * 32 + 1, (pt % 4) * TT:(pt % 4) * TT + TT]
        OT = lambda pt: ot[:, (pt % 4) * TT:(pt % 4) * TT + TT]

        with nc.Block() as blk:
            @blk.sync
            def _(sy):
                sy.dma_start(out=bd33[:], in_=bd33b).then_inc(sm["bl"], 16)
                for pt in range(NPT):
                    hp, sl = divmod(pt, 8)
                    if pt >= 4:
                        sy.wait_ge(sm["mu"], z["mu"] + pt - 3)
                    sy.dma_start(out=NA(pt),
                                 in_=num[hp * 128:(hp + 1) * 128, sl * TT:(sl + 1) * TT]
                                 ).then_inc(sm["nl"], 16)
                    if pt >= 1:
                        hp2, sl2 = divmod(pt - 1, 8)
                        sy.wait_ge(sm["mu"], z["mu"] + pt)
                        sy.dma_start(out=out[hp2 * 128:(hp2 + 1) * 128,
                                             sl2 * TT:(sl2 + 1) * TT],
                                     in_=OT(pt - 1)).then_inc(sm[f"so{(pt - 1) % 4}"], 16)
                hp2, sl2 = divmod(NPT - 1, 8)
                sy.wait_ge(sm["mu"], z["mu"] + NPT)
                sy.dma_start(out=out[hp2 * 128:(hp2 + 1) * 128,
                                     sl2 * TT:(sl2 + 1) * TT],
                             in_=OT(NPT - 1)).then_inc(sm[f"so{(NPT - 1) % 4}"], 16)
                for j in range(4):
                    sy.wait_ge(sm[f"so{j}"], z[f"so{j}"] + 16 * (NPT // 4))

            @blk.scalar
            def _(sc):
                for pt in range(NPT):
                    hp, sl = divmod(pt, 8)
                    if pt >= 4:
                        sc.wait_ge(sm["rc"], z["rc"] + 2 * (pt - 3))
                    for h in range(2):
                        sc.dma_start(out=DB(h, pt),
                                     in_=den[hp * 2 + h:hp * 2 + h + 1,
                                             sl * TT:(sl + 1) * TT]).then_inc(sm["dn"], 16)

            @blk.tensor
            def _(te):
                te.wait_ge(sm["bl"], z["bl"] + 16)
                for pt in range(NPT):
                    te.wait_ge(sm["rc"], z["rc"] + 2 * (pt + 1))
                    if pt >= 2:
                        te.wait_ge(sm["mu"], z["mu"] + pt - 1)
                    te.matmul(psR[pt % 2][:], bd33[:], RC(pt), start=True,
                              stop=True).then_inc(sm["rq"], 1)

            @blk.vector
            def _(v):
                def recs(pt):
                    for h in range(2):
                        v.wait_ge(sm["dn"], z["dn"] + 32 * (pt + 1))
                        if pt >= 4:
                            v.wait_ge(sm["rq"], z["rq"] + pt - 3)
                        with nc.allow_low_precision(
                                reason="bf16 matmul operand (broadcast lhs)"):
                            v.reciprocal(RCH(h, pt), DB(h, pt)).then_inc(sm["rc"], 1)
                recs(0)
                for pt in range(NPT):
                    if pt + 1 < NPT:
                        recs(pt + 1)
                    v.wait_ge(sm["nl"], z["nl"] + 16 * (pt + 1))
                    v.wait_ge(sm["rq"], z["rq"] + pt + 1)
                    if pt >= 4:
                        v.wait_ge(sm[f"so{pt % 4}"], z[f"so{pt % 4}"]
                                  + 16 * ((pt - 4) // 4 + 1))
                    v.tensor_tensor(OT(pt), NA(pt), psR[pt % 2][:], ALU.mult
                                    ).then_inc(sm["mu"], 1)

        sm.set_total("bl", z["bl"] + 16)
        sm.set_total("nl", z["nl"] + 16 * NPT)
        sm.set_total("dn", z["dn"] + 32 * NPT)
        sm.set_total("rc", z["rc"] + 2 * NPT)
        sm.set_total("rq", z["rq"] + NPT)
        sm.set_total("mu", z["mu"] + NPT)
        for j in range(4):
            sm.set_total(f"so{j}", z[f"so{j}"] + 16 * (NPT // 4))
    kb.barrier()


def emit_glu(kb, *, W1p, b1p_col, x, hid_out, name="gl"):
    nc = kb.nc
    import os
    NB2 = int(os.environ.get("GLU_NB2", "22"))
    wid = lambda j: min(128, 2730 - 128 * j)
    off = {}
    o = 0
    for j in range(NB2):
        off[j] = o
        o += 2 * wid(j)
    DT = 5460
    DTP = 5504
    sm = SemSet(kb, ["bl"] + [f"sh{j}" for j in range(4)],
                ["mm", "ge", "ev"], ["wl", "xl0", "xl1"])
    z = sm.z
    ntt = NTT

    with ExitStack() as ctx:
        E = ctx.enter_context
        wa = E(nc.sbuf_tensor(name + "_wa", [128, NCHUNK * DTP], BF16))
        xa = E(nc.sbuf_tensor(name + "_xa", [128, 2 * NCHUNK * TT], BF16))
        bc = E(nc.sbuf_tensor(name + "_bc", [128, 2 * NB2], F32))
        gl = E(nc.sbuf_tensor(name + "_gl", [128, 2 * TT], F32))
        ob = E(nc.sbuf_tensor(name + "_ob", [128, 4 * TT], BF16))
        psA = [E(nc.psum_tensor(name + f"_psA{p}", [128, TT], F32)) for p in range(2)]
        psB = [E(nc.psum_tensor(name + f"_psB{p}", [128, TT], F32)) for p in range(2)]
        XT = lambda p, c: xa[:, (p * NCHUNK + c) * TT:(p * NCHUNK + c) * TT + TT]
        OT = lambda g_: ob[:, (g_ % 4) * TT:(g_ % 4) * TT + TT]
        GL = lambda g_: gl[:, (g_ % 2) * TT:(g_ % 2) * TT + TT]
        xlval = lambda t: z["xl" + str(t % 2)] + 16 * NCHUNK * (t // 2 + 1)
        shval = lambda g_: z["sh" + str(g_ % 4)] + 16 * (g_ // 4 + 1)

        with nc.Block() as blk:
            @blk.gpsimd
            def _(g):
                for i in range(NCHUNK):
                    g.dma_start(out=wa[:, i * DTP:i * DTP + DT],
                                in_=W1p[i * 128:(i + 1) * 128, :]).then_inc(sm["wl"], 16)
                for t in range(ntt):
                    p = t % 2
                    if t >= 2:
                        g.wait_ge(sm["mm"], z["mm"] + 2 * NB2 * (t - 1))
                    for c in range(NCHUNK):
                        g.dma_start(out=XT(p, c),
                                    in_=x[c * 128:(c + 1) * 128, t * TT:(t + 1) * TT]
                                    ).then_inc(sm["xl" + str(p)], 16)

            @blk.scalar
            def _(sc):
                for j in range(NB2):
                    sc.dma_start(out=bc[0:wid(j), 2 * j:2 * j + 1],
                                 in_=b1p_col[off[j]:off[j] + wid(j), 0:1]).then_inc(sm["bl"], 16)
                    sc.dma_start(out=bc[0:wid(j), 2 * j + 1:2 * j + 2],
                                 in_=b1p_col[off[j] + wid(j):off[j] + 2 * wid(j), 0:1]
                                 ).then_inc(sm["bl"], 16)
                for t in range(ntt):
                    for j in range(NB2):
                        g_ = t * NB2 + j
                        sc.wait_ge(sm["mm"], z["mm"] + 2 * g_ + 2)
                        if g_ >= 2:
                            sc.wait_ge(sm["ev"], z["ev"] + g_ - 1)
                        import os as _os
                        _f = AF.Identity if _os.environ.get("GLU_NOACT") else AF.Gelu
                        sc.activation(GL(g_)[0:wid(j), :], psB[g_ % 2][0:wid(j), :], _f,
                                      bias=bc[0:wid(j), 2 * j + 1:2 * j + 2], scale=1.0
                                      ).then_inc(sm["ge"], 1)

            @blk.tensor
            def _(te):
                te.wait_ge(sm["wl"], z["wl"] + 16 * NCHUNK)
                te.wait_ge(sm["bl"], z["bl"] + 16 * 2 * NB2)
                for t in range(ntt):
                    p = t % 2
                    te.wait_ge(sm["xl" + str(p)], xlval(t))
                    for j in range(NB2):
                        g_ = t * NB2 + j
                        w = wid(j)
                        if g_ >= 2:
                            te.wait_ge(sm["ge"], z["ge"] + g_ - 1)
                            te.wait_ge(sm["ev"], z["ev"] + g_ - 1)
                        for i in range(NCHUNK):
                            ii = te.matmul(psA[g_ % 2][0:w, :],
                                           wa[:, i * DTP + off[j]: i * DTP + off[j] + w],
                                           XT(p, i), start=(i == 0), stop=(i == NCHUNK - 1))
                        ii.then_inc(sm["mm"], 1)
                        for i in range(NCHUNK):
                            ii = te.matmul(psB[g_ % 2][0:w, :],
                                           wa[:, i * DTP + off[j] + w: i * DTP + off[j] + 2 * w],
                                           XT(p, i), start=(i == 0), stop=(i == NCHUNK - 1))
                        ii.then_inc(sm["mm"], 1)

            @blk.vector
            def _(v):
                for t in range(ntt):
                    for j in range(NB2):
                        g_ = t * NB2 + j
                        w = wid(j)
                        v.wait_ge(sm["ge"], z["ge"] + g_ + 1)
                        if g_ >= 4:
                            v.wait_ge(sm["sh" + str(g_ % 4)], shval(g_ - 4))
                        import os as _os
                        if _os.environ.get("GLU_NOSTT"):
                            v.tensor_copy(OT(g_)[0:w, :], psA[g_ % 2][0:w, :]).then_inc(sm["ev"], 1)
                        else:
                            v.scalar_tensor_tensor(OT(g_)[0:w, :], psA[g_ % 2][0:w, :],
                                                   bc[0:w, 2 * j:2 * j + 1], GL(g_)[0:w, :],
                                                   ALU.add, ALU.mult).then_inc(sm["ev"], 1)

            @blk.sync
            def _(sy):
                for t in range(ntt):
                    for j in range(NB2):
                        g_ = t * NB2 + j
                        w = wid(j)
                        sy.wait_ge(sm["ev"], z["ev"] + g_ + 1)
                        sy.dma_start(out=hid_out[128 * j:128 * j + w, t * TT:(t + 1) * TT],
                                     in_=OT(g_)[0:w, :]).then_inc(sm["sh" + str(g_ % 4)], 16)
                for jj in range(4):
                    nj = (ntt * NB2 - jj + 3) // 4
                    sy.wait_ge(sm["sh" + str(jj)], z["sh" + str(jj)] + 16 * nj)

        sm.set_total("wl", z["wl"] + 16 * NCHUNK)
        sm.set_total("bl", z["bl"] + 16 * 2 * NB2)
        for p in range(2):
            sm.set_total("xl" + str(p), z["xl" + str(p)] + 16 * NCHUNK * ((ntt + 1 - p) // 2))
        ntile = ntt * NB2
        sm.set_total("mm", z["mm"] + 2 * ntile)
        sm.set_total("ge", z["ge"] + ntile)
        sm.set_total("ev", z["ev"] + ntile)
        for jj in range(4):
            sm.set_total(f"sh{jj}", z[f"sh{jj}"] + 16 * ((ntile - jj + 3) // 4))
    kb.barrier()


SCRATCH = {
    "xn0_f": ([D, N], F32), "xn0_b": ([D, N], BF16),
    "wqp_b": ([D, D], BF16), "wkp_b": ([D, D], BF16),
    "pvt": ([N, D], BF16), "qp": ([N, D], BF16), "ekk": ([N, D], BF16),
    "rs": ([1, 16], F32), "kvv": ([64, 16 * 65], BF16), "pay": ([D, N], BF16),
    "x1a": ([D, N], F32), "y1b": ([D, N], BF16),
    "mq": ([D, N], BF16), "mk": ([D, N], BF16), "mv": ([N, D], BF16),
    "negc0": ([128, 16], F32), "msay": ([D, N], BF16),
    "y2f": ([D, N], F32), "y3f": ([D, N], F32), "y3b": ([D, N], BF16),
    "ohT": ([8, N], F32), "hid": ([4096, N], BF16),
    "x1f": ([D, N], F32), "xn1_f": ([D, N], F32), "xn1_b": ([D, N], BF16),
    "aq": ([D, N], BF16), "ak": ([D, N], BF16), "avt": ([N, D], BF16),
    "aqr": ([D, N], BF16), "akr": ([D, N], BF16),
    "negc1": ([128, 16], F32), "dinv": ([16, N], F32), "l1ay": ([D, N], BF16),
    "x2a": ([D, N], F32),
    "lq": ([D, N], BF16), "lk": ([D, N], BF16), "lv": ([N, D], BF16),
    "qiv": ([16, N], F32), "kiv": ([16, N], F32),
    "kivT": ([128, 512], F32), "lstab": ([128, 6], F32),
    "lnm": ([D, N], BF16), "lden": ([16, N], F32),
    "lqr": ([D, N], BF16), "lkr": ([D, N], BF16), "lay": ([D, N], BF16),
    "ub": ([D, N], BF16), "hid2": ([2730, N], BF16),
    "out_f": ([D, N], F32),
}
for _e in range(8):
    SCRATCH[f"acc{_e}"] = ([D, N], F32)
for _z in ["zAf", "zBf", "zCf"]:
    SCRATCH[_z] = ([D, N], F32)
for _z in ["zAb", "zBb", "zCb"]:
    SCRATCH[_z] = ([D, N], BF16)

INPUTS_SHAPES = {}


def T(kb, name, make=False):
    shape, dt = SCRATCH[name]
    if kb.fused:
        if name == "out_f":
            return kb.dout(name, shape, dt)
        return kb.dtmp(name, shape, dt)
    return kb.dout(name, shape, dt) if make else kb.din(name, shape, dt)


def W(kb, host, name):
    arr = host[name]
    return kb.din(name, list(arr.shape), F32)


def WS(kb, host, name, e):
    key = f"{name}_s{e}"
    if key not in host:
        host[key] = np.ascontiguousarray(host[name][e])
    return kb.din(key, list(host[key].shape), F32)


GELU_T = AF.Gelu_apprx_tanh


def sections(host):
    H = lambda kb, n: W(kb, host, n)

    def HB(kb, n):
        arr = host[n]
        return kb.din(n, list(arr.shape), BF16)
    secs = []

    def g(name, *fns):
        secs.append((name, list(fns)))

    g("ln0", lambda kb: emit_ln(kb, src=kb.din("x0f", [D, N]),
                                out_f=T(kb, "xn0_f", 1), out_b=T(kb, "xn0_b", 1), name="ln0"))
    g("pfold",
      lambda kb: emit_proj_feat(kb, W=H(kb, "w_p_Wq_nat"), Dout=D, x=H(kb, "w_p_P"),
                                x_f32=True, out_b=T(kb, "wqp_b", 1), NTOK=D, name="fq"),
      lambda kb: emit_proj_feat(kb, W=H(kb, "w_p_Wk_nat"), Dout=D, x=H(kb, "w_p_P"),
                                x_f32=True, out_b=T(kb, "wkp_b", 1), NTOK=D, name="fk"))
    g("pfeat",
      lambda kb: emit_proj_tok(kb, W=H(kb, "w_p_WvT"), Dout=D, x=T(kb, "xn0_b"),
                               x_f32=False, out=T(kb, "pvt", 1),
                               bias_row=H(kb, "w_p_bv_row"), name="pv"),
      lambda kb: emit_perf_qk(kb, kind="q", WT=H(kb, "w_p_WqT"), WP=T(kb, "wqp_b"),
                              P=H(kb, "w_p_P"), b_row=H(kb, "w_p_bq_row"),
                              b_col=H(kb, "w_p_bq_col"), x=T(kb, "xn0_b"),
                              out=T(kb, "qp", 1), name="pfq"),
      lambda kb: emit_perf_qk(kb, kind="k", WT=H(kb, "w_p_WkT"), WP=T(kb, "wkp_b"),
                              P=H(kb, "w_p_P"), b_row=H(kb, "w_p_bk_row"),
                              b_col=H(kb, "w_p_bk_col"), x=T(kb, "xn0_b"),
                              out=T(kb, "ekk", 1), rs_out=T(kb, "rs", 1),
                              ident=H(kb, "c_id32"), name="pfk"))
    g("pav",
      lambda kb: emit_perf_kv(kb, ek=T(kb, "ekk"), v_tok=T(kb, "pvt"), rs_row=T(kb, "rs"),
                              kv_out=T(kb, "kvv", 1), name="pk2"),
      lambda kb: emit_perf_qkv(kb, qp=T(kb, "qp"), kv=T(kb, "kvv"), out_y=T(kb, "pay", 1),
                               ident16=H(kb, "c_id32"), name="pq3"))
    g("pout",
      lambda kb: emit_proj_feat(kb, W=H(kb, "w_p_WoT"), Dout=D, x=T(kb, "pay"),
                                x_f32=False, out_f=T(kb, "x1a", 1),
                                bias_col=H(kb, "w_p_bo_col"), resid=kb.din("x0f", [D, N]),
                                name="po"),
      lambda kb: emit_ln(kb, src=T(kb, "xn0_f"), out_b=T(kb, "y1b", 1),
                         gain_col=H(kb, "w_m_ln1w_col"), bias_col=H(kb, "w_m_ln1b_col"),
                         name="ln1"))
    g("mqkv",
      lambda kb: emit_proj_feat(kb, W=H(kb, "w_m_qkv")[:, 0:D], Dout=D, x=T(kb, "y1b"),
                                x_f32=False, out_b=T(kb, "mq", 1), name="pq"),
      lambda kb: emit_proj_feat(kb, W=H(kb, "w_m_qkv")[:, D:2 * D], Dout=D, x=T(kb, "y1b"),
                                x_f32=False, out_b=T(kb, "mk", 1), name="pk"),
      lambda kb: emit_proj_tok(kb, W=H(kb, "w_m_qkv")[:, 2 * D:3 * D], Dout=D,
                               x=T(kb, "y1b"), x_f32=False, out=T(kb, "mv", 1), name="pv"))
    g("mattn",
      lambda kb: emit_attn_norms(kb, qf=T(kb, "mq"), kf=T(kb, "mk"),
                                 negc_out=T(kb, "negc0", 1), scale=0.125,
                                 bd16=H(kb, "c_bd16"), ident=H(kb, "c_id32"), name="an"),
      lambda kb: emit_attn_dense(kb, kind="msa", qf=T(kb, "mq"), kf=T(kb, "mk"),
                                 v_tok=T(kb, "mv"), negc=T(kb, "negc0"), scale=0.125,
                                 bd2T=H(kb, "c_bd33T"), out_y=T(kb, "msay", 1), name="ma"))
    g("mout",
      lambda kb: emit_proj_feat(kb, W=H(kb, "w_m_proj"), Dout=D, x=T(kb, "msay"),
                                x_f32=False, out_f=T(kb, "y2f", 1), resid=T(kb, "xn0_f"),
                                name="po"),
      lambda kb: emit_ln(kb, src=T(kb, "y2f"), out_f=T(kb, "y3f", 1),
                         out_b=T(kb, "y3b", 1), gain_col=H(kb, "w_m_ln2w_col"),
                         bias_col=H(kb, "w_m_ln2b_col"), name="ln2"))
    g("gate", lambda kb: emit_moe_gate(kb, gate_w=H(kb, "w_m_gate"),
                                       gate_b_row=H(kb, "w_m_gateb_row"), x=T(kb, "y3f"),
                                       ohT_out=T(kb, "ohT", 1), ident=H(kb, "c_id32"),
                                       name="mg"))
    for e in range(8):
        src_acc = "y2f" if e == 0 else f"acc{e - 1}"
        g(f"moe{e}",
          lambda kb, e=e: emit_proj_feat(kb, W=WS(kb, host, "w_m_w1", e), Dout=4096,
                                         x=T(kb, "y3b"), x_f32=False, out_b=T(kb, "hid", 1),
                                         act=GELU_T, mask_mul=T(kb, "ohT")[e:e + 1, :],
                                         name=f"h{e}"),
          lambda kb, e=e, s=src_acc: emit_proj_feat(kb, W=WS(kb, host, "w_m_w2", e), Dout=D,
                                                    Din=4096, x=T(kb, "hid"), x_f32=False,
                                                    out_f=T(kb, f"acc{e}", 1),
                                                    resid=T(kb, s), name=f"o{e}"))
    g("x1",
      lambda kb: emit_add(kb, srcs=[T(kb, "x1a"), T(kb, "acc7")], out_f=T(kb, "x1f", 1),
                          name="a1"),
      lambda kb: emit_ln(kb, src=T(kb, "x1f"), out_f=T(kb, "xn1_f", 1),
                         out_b=T(kb, "xn1_b", 1), name="lt1"))
    g("aqkv",
      lambda kb: emit_proj_feat(kb, W=H(kb, "w_a_qkv")[:, 0:D], Dout=D, x=T(kb, "xn1_b"),
                                x_f32=False, out_b=T(kb, "aq", 1), name="pq"),
      lambda kb: emit_proj_feat(kb, W=H(kb, "w_a_qkv")[:, D:2 * D], Dout=D,
                                x=T(kb, "xn1_b"), x_f32=False, out_b=T(kb, "ak", 1),
                                name="pk"),
      lambda kb: emit_proj_tok(kb, W=H(kb, "w_a_qkv")[:, 2 * D:3 * D], Dout=D,
                               x=T(kb, "xn1_b"), x_f32=False, out=T(kb, "avt", 1),
                               name="pv"))
    g("arope",
      lambda kb: emit_rope(kb, src=T(kb, "aq"), out=T(kb, "aqr", 1),
                           tab1=H(kb, "c_l1a_cos"), tab2=H(kb, "c_l1a_sin"),
                           qs2_col=H(kb, "c_ones_c128"), sgn_col=H(kb, "c_sgn32"), name="rq"),
      lambda kb: emit_rope(kb, src=T(kb, "ak"), out=T(kb, "akr", 1),
                           tab1=H(kb, "c_l1a_cos"), tab2=H(kb, "c_l1a_sin"),
                           qs2_col=H(kb, "c_ones_c128"), sgn_col=H(kb, "c_sgn32"), name="rk"),
      lambda kb: emit_attn_norms(kb, qf=T(kb, "aq"), kf=T(kb, "ak"),
                                 negc_out=T(kb, "negc1", 1), scale=0.125,
                                 bd16=H(kb, "c_bd16"), ident=H(kb, "c_id32"), name="an"))
    g("aden", lambda kb: emit_attn_dense(kb, kind="l1den", qf=T(kb, "aq"), kf=T(kb, "ak"),
                                         v_tok=None, negc=T(kb, "negc1"), scale=0.125,
                                         bd2T=H(kb, "c_bd2T"),
                                         den_io=T(kb, "dinv", 1), name="ad"))
    g("anum", lambda kb: emit_attn_dense(kb, kind="l1num", qf=T(kb, "aqr"), kf=T(kb, "akr"),
                                         v_tok=T(kb, "avt"), negc=T(kb, "negc1"),
                                         scale=0.125, den_io=T(kb, "dinv"),
                                         bd2T=H(kb, "c_bd2T"),
                                         out_y=T(kb, "l1ay", 1),
                                         masks=H(kb, "c_l1a_mask"), name="anu"))
    g("aout", lambda kb: emit_proj_feat(kb, W=H(kb, "w_a_out"), Dout=D, x=T(kb, "l1ay"),
                                        x_f32=False, out_f=T(kb, "x2a", 1),
                                        bias_col=H(kb, "w_a_bout_col"),
                                        resid=T(kb, "x1f"), name="ao"))
    g("lstab", lambda kb: emit_lstab(
        kb, qs2_6=[WS(kb, host, "w_f_qs2_col", i) for i in range(6)],
        ks2_6=[WS(kb, host, "w_f_ks2_col", i) for i in range(6)],
        ident=H(kb, "c_id32"), out=T(kb, "lstab", 1), name="ls"))
    zf, zb = "xn1_f", "xn1_b"
    ring = ["zA", "zB", "zC"]
    for i in range(6):
        avail = [r for r in ring if r + "f" != zf]
        zm, zo = avail[0], avail[1]
        zmf, zmb = zm + "f", zm + "b"
        zof, zob = zo + "f", zo + "b"
        g(f"lf{i}a",
          lambda kb, i=i, zb=zb: emit_proj_feat(kb, W=WS(kb, host, "w_f_qkv", i)[:, 0:D], Dout=D,
                                                x=T(kb, zb), x_f32=False, shift=True,
                                                out_b=T(kb, "lq", 1), name="pq"),
          lambda kb, i=i, zb=zb: emit_proj_feat(kb, W=WS(kb, host, "w_f_qkv", i)[:, D:2 * D],
                                                Dout=D, x=T(kb, zb), x_f32=False,
                                                shift=True, out_b=T(kb, "lk", 1), name="pk"),
          lambda kb, i=i, zb=zb: emit_proj_tok(kb, W=WS(kb, host, "w_f_qkv", i)[:, 2 * D:3 * D],
                                               Dout=D, x=T(kb, zb), x_f32=False,
                                               shift=True, out=T(kb, "lv", 1), name="pv"))
        g(f"lf{i}b",
          lambda kb, i=i: emit_l2norms(kb, qf=T(kb, "lq"), kf=T(kb, "lk"),
                                       qinv_out=T(kb, "qiv", 1), kinv_out=T(kb, "kiv", 1),
                                       kivT_out=T(kb, "kivT", 1), ident=H(kb, "c_id32"),
                                       bd16=H(kb, "c_bd16"), name="l2"),
          lambda kb, i=i: emit_rope(kb, src=T(kb, "lq"), out=T(kb, "lqr", 1),
                                    tab1=H(kb, "c_lf_cs"), tab2=H(kb, "c_lf_ss"),
                                    qs2_col=WS(kb, host, "w_f_qs2_col", i),
                                    sgn_col=H(kb, "c_sgn32"), name="rq"),
          lambda kb, i=i: emit_rope(kb, src=T(kb, "lk"), out=T(kb, "lkr", 1),
                                    tab1=H(kb, "c_lf_ci"), tab2=H(kb, "c_lf_si"),
                                    qs2_col=WS(kb, host, "w_f_ks2_col", i),
                                    sgn_col=H(kb, "c_sgn32"), name="rk"))
        g(f"lf{i}c",
          lambda kb, i=i: emit_lattn(kb, qop=T(kb, "lqr"), kop=T(kb, "lkr"),
                                     v_tok=T(kb, "lv"), qinv=T(kb, "qiv"),
                                     kivT=T(kb, "kivT"),
                                     ngc_src=T(kb, "lstab"), lidx=i,
                                     bd2T=HB(kb, "c_bd2b"),
                                     out_y=T(kb, "lnm", 1), den_out=T(kb, "lden", 1),
                                     masks=H(kb, "c_lf_mask"),
                                     name="la"),
          lambda kb, i=i: emit_lnorm(kb, num=T(kb, "lnm"), den=T(kb, "lden"),
                                     out=T(kb, "lay", 1), bd33b=HB(kb, "c_bd33b"),
                                     name="lno"),
          lambda kb, i=i, zf=zf, zmf=zmf, zmb=zmb: emit_proj_feat(
              kb, W=WS(kb, host, "w_f_Wo", i), Dout=D, x=T(kb, "lay"), x_f32=False,
              out_f=T(kb, zmf, 1), out_b=T(kb, zmb, 1), resid=T(kb, zf), name="wo"))
        g(f"lf{i}d",
          lambda kb, i=i, zmf=zmf: emit_ln(kb, src=T(kb, zmf), out_b=T(kb, "ub", 1),
                                           gain_col=WS(kb, host, "w_f_lng_col", i),
                                           bias_col=WS(kb, host, "w_f_lnb_col", i),
                                           shift=True, name="lnf"),
          lambda kb, i=i: emit_glu(kb, W1p=WS(kb, host, "w_f_W1p", i), b1p_col=WS(kb, host, "w_f_b1p_col", i),
                                   x=T(kb, "ub"), hid_out=T(kb, "hid2", 1), name="glu"),
          lambda kb, i=i, zmf=zmf, zof=zof, zob=zob: emit_proj_feat(
              kb, W=WS(kb, host, "w_f_W2", i), Dout=D, Din=2730, x=T(kb, "hid2"), x_f32=False,
              bias_col=WS(kb, host, "w_f_b2_col", i), out_f=T(kb, zof, 1), out_b=T(kb, zob, 1),
              resid=T(kb, zmf), name="w2"))
        zf, zb = zof, zob
    g("fin", lambda kb, zf=zf: emit_add(kb, srcs=[T(kb, "x2a"), T(kb, zf)],
                                        out_f=T(kb, "out_f", 1), name="fo"))
    return secs


def _plan_io(host):
    import gc
    secs = sections(host)
    plan = []
    for gname, fns in secs:
        kb = KB()
        for fn in fns:
            fn(kb)
        reads = [n for n, k in kb.kinds.items() if k == "in"]
        writes = [n for n, k in kb.kinds.items() if k == "out"]
        kb.close()
        del kb
        gc.collect()
        plan.append((gname, reads, writes))
    last_use = {}
    for gi, (gname, reads, writes) in enumerate(plan):
        for n in reads:
            last_use[n] = gi
    return plan, last_use


def kernel_multi(**inputs):
    import gc
    consts = make_consts()
    host = make_host_arrays(inputs)
    host.update(consts)
    xs = np.asarray(inputs["L0_attn_query"])
    percore = [{"x0f": np.ascontiguousarray(xs[c].T)} for c in range(8)]
    plan, last_use = _plan_io(host)
    secs = sections(host)
    for gi, (gname, fns) in enumerate(secs):
        kb = KB()
        for fn in fns:
            fn(kb)
        kb.close()
        in_maps = []
        for c in range(8):
            m = {}
            for n, kind in kb.kinds.items():
                if kind == "in":
                    m[n] = percore[c][n] if n in percore[c] else host[n]
            in_maps.append(m)
        res = run_bass_kernel_spmd(kb.nc, in_maps, core_ids=list(range(8)))
        for c in range(8):
            for n, arr in res.results[c].items():
                percore[c][n] = np.asarray(arr)
        del res, kb, in_maps
        try:
            import jax
            jax.clear_caches()
        except Exception:
            pass
        for c in range(8):
            for n in list(percore[c].keys()):
                if n != "out_f" and last_use.get(n, 10 ** 9) <= gi:
                    del percore[c][n]
        gc.collect()
    out = np.stack([np.asarray(percore[c]["out_f"], dtype=np.float32).T for c in range(8)])
    return out


def build_fused(host):
    kb = KB(fused=True)
    kb.din("x0f", [D, N])
    secs = sections(host)
    for gi, (gname, fns) in enumerate(secs):
        if gi == len(secs) // 2:
            kb.gen = 1
        for fn in fns:
            fn(kb)
    kb.close()
    return kb


def kernel(**inputs):
    consts = make_consts()
    host = make_host_arrays(inputs)
    host.update(consts)
    xs = np.asarray(inputs["L0_attn_query"])
    kb = build_fused(host)
    in_maps = []
    for c in range(8):
        m = {"x0f": np.ascontiguousarray(xs[c].T)}
        for n, kind in kb.kinds.items():
            if kind == "in" and n != "x0f":
                m[n] = host[n]
        in_maps.append(m)
    res = run_bass_kernel_spmd(kb.nc, in_maps, core_ids=list(range(8)))
    out = np.stack([np.asarray(res.results[c]["out_f"], dtype=np.float32).T
                    for c in range(8)])
    return out
